# version 23
# baseline (speedup 1.0000x reference)
import numpy as np
import ml_dtypes
from contextlib import ExitStack

import concourse.bass as bass
import concourse.mybir as mybir
from concourse.bass_utils import run_bass_kernel_spmd

F32 = mybir.dt.float32
BF16 = mybir.dt.bfloat16
OP = mybir.AluOpType
AF = mybir.ActivationFunctionType
AX = mybir.AxisListType.X

D = 2048
B = 2
T = 16384
NCORES = 8
CPB = 4
R = (B * T) // NCORES
NSUP = 8
TSUP = R // NSUP
NSUB = TSUP // 128
NT = NSUP * NSUB
NCH = D // 128
NNC = D // 512
INV_D = 1.0 / D
LN_EPS = 1e-5
GN_EPS = 1e-5
CLAMP_LO = -87.0
CLAMP_INV = 23.0258509299

ENGINES = ("pe", "act", "dve", "pool", "sp")
BLOCK_ATTR = {"pe": "tensor", "act": "scalar", "dve": "vector", "pool": "gpsimd", "sp": "sync"}


class Prog:

    def __init__(self, nc, stack):
        self.nc = nc
        self.stack = stack
        self.sem = {}
        for name in ("pe", "act", "dve", "pool", "sp", "cc"):
            self.sem[name] = stack.enter_context(nc.semaphore(name))
        self.cnt = {k: 0 for k in self.sem}
        self.actions = {e: [] for e in ENGINES}
        self.clock = {e: {} for e in ENGINES}
        self.res = {}
        self.nwaits = 0
        self.nsteps = 0

    def _dma_sem(self, key):
        name = "d_" + key
        if name not in self.sem:
            self.sem[name] = self.stack.enter_context(self.nc.semaphore(name))
            self.cnt[name] = 0
        return name

    def _record(self, eng, fn, sem_name, inc, reads, writes, kind):
        deps = []
        for r in reads:
            rec = self.res.get(r)
            if rec:
                deps += rec["w"]
        for w in writes:
            rec = self.res.get(w)
            if rec:
                deps += rec["w"] + rec["r"]
        clk = self.clock[eng]
        need = {}
        merge = []
        for (s, cval, snap) in deps:
            if kind == "compute" and s == eng and eng == "pe":
                continue
            if kind == "dma" and s == sem_name:
                continue
            if clk.get(s, 0) >= cval:
                continue
            if need.get(s, 0) < cval:
                need[s] = cval
            merge.append(snap)
        for snap in merge:
            for ks, kv in snap.items():
                if clk.get(ks, 0) < kv:
                    clk[ks] = kv
        for s, v in need.items():
            if clk.get(s, 0) < v:
                clk[s] = v
        waits = sorted(need.items())
        self.nwaits += len(waits)
        self.nsteps += 1

        self.cnt[sem_name] += inc
        count = self.cnt[sem_name]
        snap = dict(clk)
        snap[sem_name] = count
        evt = (sem_name, count, snap)
        for w in writes:
            self.res[w] = {"w": [evt], "r": []}
        for r in reads:
            rec = self.res.get(r)
            if rec is not None:
                rec["r"].append(evt)
        self.actions[eng].append((waits, fn, sem_name, inc))
        return evt

    def comp(self, eng, fn, reads=(), writes=()):
        return self._record(eng, fn, eng, 1, reads, writes, "compute")

    def dma(self, eng, fn, reads=(), writes=(), key=None):
        if key is None:
            key = writes[0] if writes else reads[0]
        return self._record(eng, fn, self._dma_sem(key), 16, reads, writes, "dma")

    def cc(self, fn, reads=(), writes=()):
        return self._record("pool", fn, "cc", 1, reads, writes, "cc")

    def barrier(self):
        for eng in ENGINES:
            clk = self.clock[eng]
            waits = []
            for s, c in self.cnt.items():
                if c > 0 and clk.get(s, 0) < c:
                    waits.append((s, c))
                    clk[s] = c
            if waits:
                self.actions[eng].append((sorted(waits), None, None, 0))

    def finalize(self, block):
        self.barrier()
        sems = self.sem
        for eng in ENGINES:
            attr = BLOCK_ATTR[eng]
            acts = self.actions[eng]

            def make_body(acts=acts):
                def body(section):
                    for waits, fn, sem_name, inc in acts:
                        for s, v in waits:
                            section.wait_ge(sems[s], v)
                        if fn is not None:
                            last = fn(section)
                            if inc:
                                last.then_inc(sems[sem_name], inc)
                return body

            getattr(block, attr)(make_body())


def build_program(debug_taps=()):
    nc = bass.Bass("TRN2", target_bir_lowering=False)

    x_in = nc.dram_tensor("x", [R, D], F32, kind="ExternalInput").ap()
    wrT_in = nc.dram_tensor("wrT", [D, D], F32, kind="ExternalInput").ap()
    wkT_in = nc.dram_tensor("wkT", [D, D], F32, kind="ExternalInput").ap()
    wvT_in = nc.dram_tensor("wvT", [D, D], F32, kind="ExternalInput").ap()
    woT_in = nc.dram_tensor("woT", [D, D], F32, kind="ExternalInput").ap()
    wffn_in = nc.dram_tensor("wffn", [D, D], F32, kind="ExternalInput").ap()
    bias_in = nc.dram_tensor("bias", [1, D], F32, kind="ExternalInput").ap()
    gnw_in = nc.dram_tensor("gnw", [128, NCH], F32, kind="ExternalInput").ap()
    gnb_in = nc.dram_tensor("gnb", [128, NCH], F32, kind="ExternalInput").ap()
    decay_in = nc.dram_tensor("decay", [128, NCH], F32, kind="ExternalInput").ap()
    pos_in = nc.dram_tensor("pos", [128, R], F32, kind="ExternalInput").ap()
    pm_in = nc.dram_tensor("pm", [128, NCORES], F32, kind="ExternalInput").ap()
    bm_in = nc.dram_tensor("bm", [2, NCORES], F32, kind="ExternalInput").ap()
    out_ext = nc.dram_tensor("out", [R, D], F32, kind="ExternalOutput").ap()

    def spill(name, shape, dtype):
        kind = "ExternalOutput" if name in debug_taps else "Internal"
        return nc.dram_tensor(name, shape, dtype, kind=kind)

    x1_sp = spill("x1_sp", [R, D], F32)
    xn2T_sp = spill("xn2T_sp", [NSUP, 128, NCH * TSUP], BF16)
    r_sp = spill("r_sp", [NSUP, 128, NCH * TSUP], BF16)
    y1_sp = spill("y1_sp", [NSUP, NCH // 2, 128, 1024], BF16)
    rs_sp = spill("rs_sp", [NSUP, NCH // 2, 128, 1024], BF16)
    qT_sp = spill("qT_sp", [128, NCH * D], BF16)
    tot_tap = spill("tot_tap", [128, NCH], F32)
    c_tap = spill("c_tap", [128, NCH], F32)
    stat_tap = spill("stat_tap", [1, 4], F32)
    nrm_tap = spill("nrm_tap", [NSUP, 128, NCH * TSUP], BF16)
    x2_tap = spill("x2_tap", [R, D], F32)
    wk_bf = nc.dram_tensor("wk_bf", [D, D], BF16)
    wv_bf = nc.dram_tensor("wv_bf", [D, D], BF16)
    wo_bf = nc.dram_tensor("wo_bf", [D, D], BF16)
    cc1_in = nc.dram_tensor("cc1_in", [128, NCH], F32)
    cc1_out = nc.dram_tensor("cc1_out", [NCORES * 128, NCH], F32)
    cc2_in = nc.dram_tensor("cc2_in", [1, 2], F32)
    cc2_out = nc.dram_tensor("cc2_out", [NCORES, 2], F32)
    sc_bounce = nc.dram_tensor("sc_bounce", [1, 2], F32)

    ident_d = nc.inline_tensor(np.eye(128, dtype=ml_dtypes.bfloat16), name="ident_c")

    with ExitStack() as stack:
        E = stack.enter_context
        p = Prog(nc, stack)

        ident = E(nc.sbuf_tensor("ident", [128, 128], BF16))
        ones_row = E(nc.sbuf_tensor("ones_row", [1, 128], BF16))
        bias_bf = E(nc.sbuf_tensor("bias_bf", [1, D], BF16))
        logld = E(nc.sbuf_tensor("logld", [128, NCH], F32))
        carry = E(nc.sbuf_tensor("carry", [128, NCH], F32))
        c_colA = E(nc.sbuf_tensor("c_colA", [128, NCH], F32))
        c_colB = E(nc.sbuf_tensor("c_colB", [128, NCH], F32))
        moms = {nm: E(nc.sbuf_tensor("m_" + nm, [128, NCH], F32))
                for nm in ("y1", "rs", "y1sq", "rssq", "y1rs")}
        moms_s = {nm: E(nc.sbuf_tensor("ms_" + nm, [128, NCH], F32))
                  for nm in ("y1", "rs", "y1sq", "rssq", "y1rs")}
        a_col = E(nc.sbuf_tensor("a_col", [128, NCH], F32))
        nb_col = E(nc.sbuf_tensor("nb_col", [128, NCH], F32))

        qT = E(nc.sbuf_tensor("qT", [128, NCH * D], BF16))
        wrT = E(nc.sbuf_tensor("wrT_sb", [128, NCH * D], BF16))
        xn_bf = [E(nc.sbuf_tensor(f"xn_bf{i}", [128, D], BF16)) for i in range(2)]
        xn1T = [E(nc.sbuf_tensor(f"xn1T{i}", [128, D], BF16)) for i in range(2)]
        s_col = [E(nc.sbuf_tensor(f"s_col{i}", [128, 1], F32)) for i in range(2)]
        q_col = [E(nc.sbuf_tensor(f"q_col{i}", [128, 1], F32)) for i in range(2)]
        mean_c = [E(nc.sbuf_tensor(f"mean_c{i}", [128, 1], F32)) for i in range(2)]
        rstd_c = [E(nc.sbuf_tensor(f"rstd_c{i}", [128, 1], F32)) for i in range(2)]

        pb = [E(nc.psum_tensor(f"pb{i}", [128, 512], F32)) for i in range(6)]
        pt = [E(nc.psum_tensor(f"pt{i}", [128, 1024], BF16)) for i in range(2)]

        es0 = ExitStack()
        E0 = es0.enter_context
        dtmp = E0(nc.sbuf_tensor("dtmp", [128, NCH], F32))
        wnat = [E0(nc.sbuf_tensor(f"wnat{i}", [128, D], F32)) for i in range(2)]
        gq_t = [E0(nc.sbuf_tensor(f"gq_t{i}", [128, D], BF16)) for i in range(2)]
        lq_t = [E0(nc.sbuf_tensor(f"lq_t{i}", [128, D], BF16)) for i in range(2)]
        qn_bf = [E0(nc.sbuf_tensor(f"qn_bf{i}", [128, D], BF16)) for i in range(2)]
        thr_c = E0(nc.sbuf_tensor("thr_c", [128, 1], F32))
        nthr_c = E0(nc.sbuf_tensor("nthr_c", [128, 1], F32))

        p.dma("sp", lambda s: s.dma_start(out=ident[:], in_=ident_d.ap()), writes=["ident"])
        p.dma("sp", lambda s: s.dma_start(out=logld[:], in_=decay_in), writes=["logld"])
        p.dma("pool", lambda g: g.dma_start(out=bias_bf[:], in_=bias_in), writes=["bias_bf"])
        p.comp("dve", lambda v: v.memset(ones_row[:], 1.0), writes=["ones_row"])
        p.comp("dve", lambda v: v.memset(carry[:], 0.0), writes=["carry"])
        for nm in moms:
            p.comp("dve", lambda v, nm=nm: v.memset(moms[nm][:], 0.0), writes=["m_" + nm])

        p.comp("act", lambda s: s.activation(out=dtmp[:], in_=logld[:], func=AF.Exp, scale=-1.0),
               reads=["logld"], writes=["dtmp"])
        p.comp("dve", lambda v: v.tensor_scalar(out=dtmp[:], in0=dtmp[:], scalar1=1.0,
                                                scalar2=None, op0=OP.add),
               reads=["dtmp"], writes=["dtmp"])
        p.comp("dve", lambda v: v.reciprocal(out=dtmp[:], in_=dtmp[:]),
               reads=["dtmp"], writes=["dtmp"])
        p.comp("dve", lambda v: v.tensor_scalar(out=dtmp[:], in0=dtmp[:], scalar1=1e-7,
                                                scalar2=None, op0=OP.max),
               reads=["dtmp"], writes=["dtmp"])
        p.comp("act", lambda s: s.activation(out=logld[:], in_=dtmp[:], func=AF.Ln),
               reads=["dtmp"], writes=["logld"])

        def q_load(oc):
            wb, wbn = wnat[oc % 2], f"wnat{oc % 2}"
            p.dma("sp", lambda s, oc=oc, wb=wb: s.dma_start(
                out=wb[:], in_=wffn_in[oc * 128:(oc + 1) * 128, :]), writes=[wbn])

        cast_jobs = []
        for kc in range(NCH):
            cast_jobs.append((lambda g, kc=kc: g.dma_start(
                out=wrT[:, kc * D:(kc + 1) * D],
                in_=wrT_in[kc * 128:(kc + 1) * 128, :]), "wrT"))
        NCC = 8
        for (dst, src, nm) in ((wk_bf, wkT_in, "wk_bf"), (wv_bf, wvT_in, "wv_bf"),
                               (wo_bf, woT_in, "wo_bf")):
            rows = D // NCC
            for i in range(NCC):
                cast_jobs.append((lambda g, dst=dst, src=src, i=i, rows=rows: g.dma_start(
                    out=dst.ap()[i * rows:(i + 1) * rows, :],
                    in_=src[i * rows:(i + 1) * rows, :]), nm))
        cast_i = 0

        def emit_casts(njobs):
            nonlocal cast_i
            for _ in range(njobs):
                if cast_i >= len(cast_jobs):
                    return
                fn, nm = cast_jobs[cast_i]
                p.dma("pool", fn, writes=[nm])
                cast_i += 1

        q_load(0)
        for oc in range(NCH):
            if oc + 1 < NCH:
                q_load(oc + 1)
            emit_casts(1)
            wb, wbn = wnat[oc % 2], f"wnat{oc % 2}"
            gq, gqn = gq_t[oc % 2], f"gq_t{oc % 2}"
            lq, lqn = lq_t[oc % 2], f"lq_t{oc % 2}"
            qn, qnn = qn_bf[oc % 2], f"qn_bf{oc % 2}"
            p.comp("dve", lambda v, wb=wb: v.reduce_sum(
                out=thr_c[:], in_=wb[:], axis=AX, apply_absolute_value=True),
                reads=[wbn], writes=["thr_c"])
            p.comp("dve", lambda v: v.tensor_scalar(
                out=thr_c[:], in0=thr_c[:], scalar1=0.7 * INV_D, scalar2=None, op0=OP.mult),
                reads=["thr_c"], writes=["thr_c"])
            p.comp("dve", lambda v: v.tensor_scalar(
                out=nthr_c[:], in0=thr_c[:], scalar1=-1.0, scalar2=None, op0=OP.mult),
                reads=["thr_c"], writes=["nthr_c"])
            p.comp("dve", lambda v, wb=wb, gq=gq: v.tensor_scalar(
                out=gq[:], in0=wb[:], scalar1=thr_c[:], scalar2=None, op0=OP.is_gt),
                reads=[wbn, "thr_c"], writes=[gqn])
            p.comp("dve", lambda v, wb=wb, lq=lq: v.tensor_scalar(
                out=lq[:], in0=wb[:], scalar1=nthr_c[:], scalar2=None, op0=OP.is_lt),
                reads=[wbn, "nthr_c"], writes=[lqn])
            p.comp("pool", lambda g, gq=gq, lq=lq, qn=qn: g.tensor_tensor(
                out=qn[:], in0=gq[:], in1=lq[:], op=OP.subtract),
                reads=[gqn, lqn], writes=[qnn])
            for half in range(2):
                def tr_fn(t, half=half, qn=qn):
                    last = None
                    for k in range(8):
                        kc = half * 8 + k
                        last = t.transpose(out=pt[half][:, k * 128:(k + 1) * 128],
                                           in_=qn[:, kc * 128:(kc + 1) * 128],
                                           identity=ident[:])
                    return last
                p.comp("pe", tr_fn, reads=[qnn, "ident"], writes=[f"pt{half}"])
                def ev_fn(s, oc=oc, half=half):
                    dst = qT.ap().rearrange("p (k w) -> p k w", w=D)[
                        :, half * 8:(half + 1) * 8, oc * 128:(oc + 1) * 128]
                    return s.copy(out=dst, in_=pt[half].ap().rearrange(
                        "p (k w) -> p k w", w=128))
                p.comp("act", ev_fn, reads=[f"pt{half}"], writes=["qT"])

        p.dma("act", lambda s: s.dma_start(out=qT_sp.ap(), in_=qT[:]),
              reads=["qT"], writes=["qT_sp"])


        p.barrier()
        es0.close()

        es_mid = ExitStack()
        xn2T = es_mid.enter_context(nc.sbuf_tensor("xn2T", [128, NCH * TSUP], BF16))
        r_bf = es_mid.enter_context(nc.sbuf_tensor("r_bf", [128, NCH * TSUP], BF16))

        def layer_norm(x_t, xname, par):
            xnb, sC, qC, mC, rC = xn_bf[par], s_col[par], q_col[par], mean_c[par], rstd_c[par]
            xnbn, sn, qn, mn, rn = (f"xn_bf{par}", f"s_col{par}", f"q_col{par}",
                                    f"mean_c{par}", f"rstd_c{par}")
            p.comp("dve", lambda v: v.reduce_sum(out=sC[:], in_=x_t[:], axis=AX),
                   reads=[xname], writes=[sn])
            p.comp("act", lambda s: s.activation(out=xnb[:], in_=x_t[:], func=AF.Square,
                                                 accum_out=qC[:]),
                   reads=[xname], writes=[xnbn, qn])
            p.comp("dve", lambda v: v.tensor_scalar(out=mC[:], in0=sC[:],
                                                    scalar1=INV_D, scalar2=None, op0=OP.mult),
                   reads=[sn], writes=[mn])
            p.comp("dve", lambda v: v.tensor_scalar(out=qC[:], in0=qC[:],
                                                    scalar1=INV_D, scalar2=None, op0=OP.mult),
                   reads=[qn], writes=[qn])
            p.comp("dve", lambda v: v.tensor_tensor(out=sC[:], in0=mC[:],
                                                    in1=mC[:], op=OP.mult),
                   reads=[mn], writes=[sn])
            p.comp("dve", lambda v: v.tensor_tensor(out=sC[:], in0=sC[:],
                                                    in1=qC[:], op=OP.subtract),
                   reads=[sn, qn], writes=[sn])
            p.comp("dve", lambda v: v.tensor_scalar(out=sC[:], in0=sC[:],
                                                    scalar1=-1.0, scalar2=LN_EPS,
                                                    op0=OP.mult, op1=OP.add),
                   reads=[sn], writes=[sn])
            p.comp("act", lambda s: s.activation(out=sC[:], in_=sC[:], func=AF.Ln),
                   reads=[sn], writes=[sn])
            p.comp("act", lambda s: s.activation(out=rC[:], in_=sC[:], func=AF.Exp,
                                                 scale=-0.5),
                   reads=[sn], writes=[rn])
            p.comp("dve", lambda v: v.tensor_scalar(out=xnb[:], in0=x_t[:],
                                                    scalar1=mC[:], scalar2=rC[:],
                                                    op0=OP.subtract, op1=OP.mult),
                   reads=[xname, mn, rn], writes=[xnbn])

        def transpose16(par, dstname, dst_view_fn):
            xnb, xnbn = xn_bf[par], f"xn_bf{par}"
            for half in range(2):
                def tr_fn(t, half=half, xnb=xnb):
                    last = None
                    for k in range(8):
                        kc = half * 8 + k
                        last = t.transpose(out=pt[half][:, k * 128:(k + 1) * 128],
                                           in_=xnb[:, kc * 128:(kc + 1) * 128],
                                           identity=ident[:])
                    return last
                p.comp("pe", tr_fn, reads=[xnbn, "ident"], writes=[f"pt{half}"])
                def ev_fn(s, half=half):
                    return s.copy(out=dst_view_fn(half), in_=pt[half].ap().rearrange(
                        "p (k w) -> p k w", w=128))
                p.comp("act", ev_fn, reads=[f"pt{half}"], writes=[dstname])

        es1 = ExitStack()
        E1 = es1.enter_context
        x_t = [E1(nc.sbuf_tensor(f"x_t{i}", [128, D], F32)) for i in range(3)]

        def p1_A_pre(t):
            g = t * 128
            xb, xbn = x_t[t % 3], f"x_t{t % 3}"
            p.dma("sp", lambda s, g=g, xb=xb: s.dma_start(
                out=xb[:], in_=x_in[g:g + 128, :]), writes=[xbn])

        def p1_A_ln(t):
            xb, xbn = x_t[t % 3], f"x_t{t % 3}"
            layer_norm(xb, xbn, 0)

        def p1_A_tp(t):
            par = t % 2
            x1Tb, x1Tn = xn1T[par], f"xn1T{par}"
            transpose16(0, x1Tn,
                        lambda half, x1Tb=x1Tb: x1Tb[:, half * 1024:(half + 1) * 1024]
                        .rearrange("p (k w) -> p k w", w=128))

        def p1_B1a(t):
            g = t * 128
            par = t % 2
            xb, xbn = x_t[t % 3], f"x_t{t % 3}"
            x1Tb, x1Tn = xn1T[par], f"xn1T{par}"
            for n in range(NNC):
                bank, bankn = pb[n], f"pb{n}"
                def mm_fn(tt, n=n, bank=bank, x1Tb=x1Tb):
                    for kc in range(NCH):
                        tt.matmul(out=bank[:], lhsT=x1Tb[:, kc * 128:(kc + 1) * 128],
                                  rhs=qT[:, kc * D + n * 512: kc * D + (n + 1) * 512],
                                  start=(kc == 0), stop=False)
                    return tt.matmul(out=bank[:], lhsT=ones_row[:],
                                     rhs=bias_bf[0:1, n * 512:(n + 1) * 512],
                                     start=False, stop=True)
                p.comp("pe", mm_fn, reads=[x1Tn, "qT", "ones_row", "bias_bf"],
                       writes=[bankn])
                p.comp("dve", lambda v, n=n, bank=bank, xb=xb: v.scalar_tensor_tensor(
                    out=xb[:, n * 512:(n + 1) * 512], in0=bank[:],
                    scalar=0.5, in1=xb[:, n * 512:(n + 1) * 512],
                    op0=OP.mult, op1=OP.add),
                    reads=[bankn, xbn], writes=[xbn])
            p.dma("act", lambda s, g=g, xb=xb: s.dma_start(
                out=x1_sp.ap()[g:g + 128, :], in_=xb[:]),
                reads=[xbn], writes=["x1_sp"])

        def p1_B1b(t):
            xb, xbn = x_t[t % 3], f"x_t{t % 3}"
            layer_norm(xb, xbn, 1)

        def p1_B2(t):
            j = t % NSUB
            transpose16(1, "xn2T",
                        lambda half, j=j: xn2T.ap().rearrange(
                            "p (k w) -> p k w", w=TSUP)[
                            :, half * 8:(half + 1) * 8, j * 128:(j + 1) * 128])

        def p1_R(s_i):
            p.dma("act", lambda s, s_i=s_i: s.dma_start(
                out=xn2T_sp.ap()[s_i], in_=xn2T[:]), reads=["xn2T"], writes=["xn2T_sp"])
            for ec in range(NCH):
                bank, bankn = pb[4 + ec % 2], f"pb{4 + ec % 2}"
                def mmr_fn(t, ec=ec, bank=bank):
                    last = None
                    for kc in range(NCH):
                        last = t.matmul(out=bank[:],
                                        lhsT=wrT[:, kc * D + ec * 128: kc * D + (ec + 1) * 128],
                                        rhs=xn2T[:, kc * TSUP:(kc + 1) * TSUP],
                                        start=(kc == 0), stop=(kc == NCH - 1))
                    return last
                p.comp("pe", mmr_fn, reads=["wrT", "xn2T"], writes=[bankn])
                p.comp("act", lambda s, bank=bank, ec=ec: s.activation(
                    out=r_bf[:, ec * TSUP:(ec + 1) * TSUP], in_=bank[:],
                    func=AF.Sigmoid),
                    reads=[bankn], writes=["r_bf"])
            p.dma("act", lambda s, s_i=s_i: s.dma_start(
                out=r_sp.ap()[s_i], in_=r_bf[:]), reads=["r_bf"], writes=["r_sp"])

        p1_A_pre(0)
        p1_A_pre(1)
        p1_A_ln(0)
        p1_A_tp(0)
        p1_A_ln(1)
        pending_b2 = None
        for t in range(NT):
            if t + 2 < NT:
                p1_A_pre(t + 2)
            p1_B1a(t)
            emit_casts(1)
            if t + 1 < NT:
                p1_A_tp(t + 1)
            if pending_b2 is not None:
                p1_B2(pending_b2)
                pending_b2 = None
            p1_B1b(t)
            if t + 2 < NT:
                p1_A_ln(t + 2)
            if t % NSUB == NSUB - 1:
                p1_B2(t)
                p1_R(t // NSUB)
            else:
                pending_b2 = t

        wkT = qT
        wvT = wrT
        p.dma("sp", lambda s: s.dma_start(
            out=wkT.ap().rearrange("p (k w) -> p k w", w=D),
            in_=wk_bf.ap().rearrange("(k p) w -> p k w", p=128)),
            reads=["wk_bf"], writes=["qT"], key="qT_w")
        p.dma("sp", lambda s: s.dma_start(
            out=wvT.ap().rearrange("p (k w) -> p k w", w=D),
            in_=wv_bf.ap().rearrange("(k p) w -> p k w", p=128)),
            reads=["wv_bf"], writes=["wrT"], key="wrT_w")

        p.barrier()
        es1.close()

        es2 = ExitStack()
        E2 = es2.enter_context
        pos_t = E2(nc.sbuf_tensor("pos_t0", [128, TSUP], F32))
        ones_t = E2(nc.sbuf_tensor("ones_t", [128, 512], BF16))
        p.comp("dve", lambda v: v.memset(ones_t[:], 1.0), writes=["ones_t"])
        v_t = [E2(nc.sbuf_tensor(f"v_t{i}", [128, 512], F32)) for i in range(2)]
        a_t = [E2(nc.sbuf_tensor(f"a_t{i}", [128, 512], F32)) for i in range(2)]
        scale_t = [E2(nc.sbuf_tensor(f"scale_t{i}", [128, 512], F32)) for i in range(2)]
        kv_t = [E2(nc.sbuf_tensor(f"kv_t{i}", [128, 512], F32)) for i in range(2)]
        lc_t = E2(nc.sbuf_tensor("lc_t", [128, 512], F32))
        sc_t = E2(nc.sbuf_tensor("sc_t", [128, 512], F32))
        y1_grp = E2(nc.sbuf_tensor("y1_grp", [128, 512], BF16))
        rs_grp = E2(nc.sbuf_tensor("rs_grp", [128, 512], BF16))

        def p2_head(s_i):
            p.dma("sp", lambda s, s_i=s_i: s.dma_start(
                out=xn2T[:], in_=xn2T_sp.ap()[s_i]), reads=["xn2T_sp"], writes=["xn2T"])
            p.dma("sp", lambda s, s_i=s_i: s.dma_start(
                out=r_bf[:], in_=r_sp.ap()[s_i]), reads=["r_sp"], writes=["r_bf"])
            p.dma("sp", lambda s, s_i=s_i: s.dma_start(
                out=pos_t[:], in_=pos_in[:, s_i * TSUP:(s_i + 1) * TSUP]),
                writes=["pos_t0"])
            for nm in moms_s:
                p.comp("dve", lambda v, nm=nm: v.memset(moms_s[nm][:], 0.0),
                       writes=["ms_" + nm])

        def p2_A(s_i, ec):
            kbank, kbankn = pb[ec % 3], f"pb{ec % 3}"
            vbank, vbankn = pb[3 + ec % 3], f"pb{3 + ec % 3}"
            def mmk_fn(t, ec=ec, bank=kbank):
                last = None
                for kc in range(NCH):
                    last = t.matmul(out=bank[:],
                                    lhsT=wkT[:, kc * D + ec * 128: kc * D + (ec + 1) * 128],
                                    rhs=xn2T[:, kc * TSUP:(kc + 1) * TSUP],
                                    start=(kc == 0), stop=(kc == NCH - 1))
                return last
            p.comp("pe", mmk_fn, reads=["qT", "xn2T"], writes=[kbankn])
            def mmv_fn(t, ec=ec, bank=vbank):
                last = None
                for kc in range(NCH):
                    last = t.matmul(out=bank[:],
                                    lhsT=wvT[:, kc * D + ec * 128: kc * D + (ec + 1) * 128],
                                    rhs=xn2T[:, kc * TSUP:(kc + 1) * TSUP],
                                    start=(kc == 0), stop=(kc == NCH - 1))
                return last
            p.comp("pe", mmv_fn, reads=["wrT", "xn2T"], writes=[vbankn])

        def p2_B_front(s_i, ec):
            kbank, kbankn = pb[ec % 3], f"pb{ec % 3}"
            aP, aPn = a_t[ec % 2], f"a_t{ec % 2}"
            scP, scPn = scale_t[ec % 2], f"scale_t{ec % 2}"
            kvP, kvPn = kv_t[ec % 2], f"kv_t{ec % 2}"
            vbank, vbankn = pb[3 + ec % 3], f"pb{3 + ec % 3}"
            p.comp("act", lambda s, bank=vbank, ec=ec: s.activation(
                out=v_t[ec % 2][:], in_=bank[:], func=AF.Copy),
                reads=[vbankn], writes=[f"v_t{ec % 2}"])
            p.comp("dve", lambda v, ec=ec, aP=aP: v.tensor_scalar(
                out=aP[:], in0=pos_t[:], scalar1=logld[:, ec:ec + 1],
                scalar2=CLAMP_LO, op0=OP.mult, op1=OP.max),
                reads=["pos_t0", "logld"], writes=[aPn])
            p.comp("act", lambda s, aP=aP, scP=scP: s.activation(
                out=scP[:], in_=aP[:], func=AF.Exp),
                reads=[aPn], writes=[scPn])
            p.comp("dve", lambda v, ec=ec, aP=aP: v.tensor_scalar(
                out=aP[:], in0=pos_t[:], scalar1=logld[:, ec:ec + 1],
                scalar2=-CLAMP_INV, op0=OP.mult, op1=OP.max),
                reads=["pos_t0", "logld"], writes=[aPn])
            p.comp("act", lambda s, aP=aP: s.activation(
                out=aP[:], in_=aP[:], func=AF.Exp, scale=-1.0),
                reads=[aPn], writes=[aPn])
            p.comp("dve", lambda v, bank=kbank, kvP=kvP, ec=ec: v.tensor_tensor(
                out=kvP[:], in0=bank[:], in1=v_t[ec % 2][:], op=OP.mult),
                reads=[kbankn, f"v_t{ec % 2}"], writes=[kvPn])

        def p2_B_mid(s_i, ec):
            aP, aPn = a_t[ec % 2], f"a_t{ec % 2}"
            kvP, kvPn = kv_t[ec % 2], f"kv_t{ec % 2}"
            p.comp("pool", lambda g, kvP=kvP, aP=aP: g.tensor_tensor(
                out=sc_t[:], in0=kvP[:], in1=aP[:], op=OP.mult),
                reads=[kvPn, aPn], writes=["sc_t"])
            p.comp("dve", lambda v, ec=ec: v.tensor_tensor_scan(
                out=lc_t[:], data0=ones_t[:], data1=sc_t[:],
                initial=carry[:, ec:ec + 1], op0=OP.mult, op1=OP.add),
                reads=["sc_t", "carry", "ones_t"], writes=["lc_t"])
            p.comp("dve", lambda v, ec=ec: v.tensor_copy(
                out=carry[:, ec:ec + 1], in_=lc_t[:, 511:512]),
                reads=["lc_t"], writes=["carry"])

        def p2_B_tail(s_i, ec):
            grp, gslot = ec // 2, ec % 2
            scP, scPn = scale_t[ec % 2], f"scale_t{ec % 2}"
            kvP, kvPn = kv_t[ec % 2], f"kv_t{ec % 2}"
            p.comp("pool", lambda g, kvP=kvP, scP=scP: g.tensor_tensor(
                out=kvP[:], in0=lc_t[:], in1=scP[:], op=OP.mult),
                reads=["lc_t", scPn], writes=[kvPn])
            p.comp("dve", lambda v, ec=ec, kvP=kvP: v.scalar_tensor_tensor(
                out=y1_grp[:], in0=kvP[:],
                scalar=1.0, in1=r_bf[:, ec * TSUP:(ec + 1) * TSUP],
                op0=OP.bypass, op1=OP.mult, accum_out=moms_s["y1"][:, ec:ec + 1]),
                reads=[kvPn, "r_bf"], writes=["y1_grp", "ms_y1"])
            p.comp("dve", lambda v, ec=ec, scP=scP: v.scalar_tensor_tensor(
                out=rs_grp[:], in0=scP[:],
                scalar=1.0, in1=r_bf[:, ec * TSUP:(ec + 1) * TSUP],
                op0=OP.bypass, op1=OP.mult, accum_out=moms_s["rs"][:, ec:ec + 1]),
                reads=[scPn, "r_bf"], writes=["rs_grp", "ms_rs"])
            dmpb, dmpn = a_t[ec % 2], f"a_t{ec % 2}"
            p.comp("dve", lambda v, ec=ec, dmpb=dmpb: v.scalar_tensor_tensor(
                out=dmpb[:], in0=y1_grp[:],
                scalar=1.0, in1=rs_grp[:],
                op0=OP.bypass, op1=OP.mult, accum_out=moms_s["y1rs"][:, ec:ec + 1]),
                reads=["y1_grp", "rs_grp"], writes=[dmpn, "ms_y1rs"])
            p.comp("act", lambda s, ec=ec, dmpb=dmpb: s.activation(
                out=dmpb[:], in_=y1_grp[:],
                func=AF.Square, accum_out=moms_s["y1sq"][:, ec:ec + 1]),
                reads=["y1_grp"], writes=[dmpn, "ms_y1sq"])
            p.comp("act", lambda s, ec=ec, dmpb=dmpb: s.activation(
                out=dmpb[:], in_=rs_grp[:],
                func=AF.Square, accum_out=moms_s["rssq"][:, ec:ec + 1]),
                reads=["rs_grp"], writes=[dmpn, "ms_rssq"])
            p.dma("act", lambda s, s_i=s_i, grp=grp, gslot=gslot: s.dma_start(
                out=y1_sp.ap()[s_i, grp][:, gslot * 512:(gslot + 1) * 512],
                in_=y1_grp[:]),
                reads=["y1_grp"], writes=["y1_sp"])
            p.dma("act", lambda s, s_i=s_i, grp=grp, gslot=gslot: s.dma_start(
                out=rs_sp.ap()[s_i, grp][:, gslot * 512:(gslot + 1) * 512],
                in_=rs_grp[:]),
                reads=["rs_grp"], writes=["rs_sp"])
            if ec == NCH - 1:
                for nm in moms:
                    p.comp("dve", lambda v, nm=nm: v.tensor_tensor(
                        out=moms[nm][:], in0=moms[nm][:], in1=moms_s[nm][:], op=OP.add),
                        reads=["m_" + nm, "ms_" + nm], writes=["m_" + nm])

        for s_i in range(NSUP):
            p2_head(s_i)
            p2_A(s_i, 0)
            p2_A(s_i, 1)
            p2_B_front(s_i, 0)
            for ec in range(NCH):
                if ec + 2 < NCH:
                    p2_A(s_i, ec + 2)
                p2_B_mid(s_i, ec)
                if ec + 1 < NCH:
                    p2_B_front(s_i, ec + 1)
                p2_B_tail(s_i, ec)

        woT = qT
        qT3 = wrT
        p.dma("sp", lambda s: s.dma_start(
            out=woT.ap().rearrange("p (k w) -> p k w", w=D),
            in_=wo_bf.ap().rearrange("(k p) w -> p k w", p=128)),
            reads=["wo_bf"], writes=["qT"], key="qT_w")
        p.dma("sp", lambda s: s.dma_start(out=qT3[:], in_=qT_sp.ap()),
              reads=["qT_sp"], writes=["wrT"], key="wrT_w")

        gnw_c = E2(nc.sbuf_tensor("gnw_c", [128, NCH], F32))
        gnb_c = E2(nc.sbuf_tensor("gnb_c", [128, NCH], F32))
        pm_bc = E2(nc.sbuf_tensor("pm_bc", [128, NCORES], F32))
        bm_r = E2(nc.sbuf_tensor("bm_r", [2, NCORES], F32))
        ones_col_f = E2(nc.sbuf_tensor("ones_col_f", [128, 1], F32))
        ones_row_f = E2(nc.sbuf_tensor("ones_row_f", [1, 128], F32))
        tinyA = E2(nc.sbuf_tensor("tinyA", [128, 8], F32))
        tinyB = E2(nc.sbuf_tensor("tinyB", [128, 8], F32))
        m_bc = E2(nc.sbuf_tensor("m_bc", [128, 1], F32))
        r_bc = E2(nc.sbuf_tensor("r_bc", [128, 1], F32))
        tot_r2 = (moms_s["rssq"], moms_s["y1rs"])
        tot_r2n = ("ms_rssq", "ms_y1rs")
        st_kr = E2(nc.sbuf_tensor("st_kr", [2, NCORES], F32))
        st_sc = E2(nc.sbuf_tensor("st_sc", [2, 1], F32))
        mq_row = E2(nc.sbuf_tensor("mq_row", [1, 2], F32))
        nvar_sc = E2(nc.sbuf_tensor("nvar_sc", [1, 1], F32))
        rstd_sc = E2(nc.sbuf_tensor("rstd_sc", [1, 1], F32))
        p.dma("pool", lambda s: s.dma_start(out=gnw_c[:], in_=gnw_in), writes=["gnw_c"])
        p.dma("pool", lambda s: s.dma_start(out=gnb_c[:], in_=gnb_in), writes=["gnb_c"])
        p.dma("pool", lambda s: s.dma_start(out=pm_bc[:], in_=pm_in), writes=["pm_bc"])
        p.dma("pool", lambda s: s.dma_start(out=bm_r[:], in_=bm_in), writes=["bm_r"])
        p.comp("dve", lambda v: v.memset(ones_col_f[:], 1.0), writes=["ones_col_f"])
        p.comp("dve", lambda v: v.memset(ones_row_f[:], 1.0), writes=["ones_row_f"])
        p.dma("pool", lambda s: s.dma_start(out=cc1_in.ap(), in_=carry[:]),
              reads=["carry"], writes=["cc1_in"])
        if "tot_tap" in debug_taps:
            p.dma("act", lambda s: s.dma_start(out=tot_tap.ap(), in_=carry[:]),
                  reads=["carry"])
        p.cc(lambda g: g.collective_compute(
            "AllGather", OP.bypass, replica_groups=[list(range(NCORES))],
            ins=[cc1_in.ap()], outs=[cc1_out.ap()]),
            reads=["cc1_in"], writes=["cc1_out"])
        p.comp("dve", lambda v: v.memset(c_colA[:], 0.0), writes=["c_colA"])
        cur, curn = c_colA, "c_colA"
        for rank in range(NCORES):
            trb, trn = tot_r2[rank % 2], tot_r2n[rank % 2]
            p.dma("pool", lambda s, rank=rank, trb=trb: s.dma_start(
                out=trb[:], in_=cc1_out.ap()[rank * 128:(rank + 1) * 128, :]),
                reads=["cc1_out"], writes=[trn])
            nxt, nxtn = ((c_colB, "c_colB") if cur is c_colA else (c_colA, "c_colA"))
            p.comp("dve", lambda v, rank=rank, cur=cur, nxt=nxt, trb=trb:
                   v.scalar_tensor_tensor(
                out=nxt[:], in0=trb[:], scalar=pm_bc[:, rank:rank + 1], in1=cur[:],
                op0=OP.mult, op1=OP.add),
                reads=[trn, "pm_bc", curn], writes=[nxtn])
            cur, curn = nxt, nxtn
        C_col, C_coln = cur, curn
        if "c_tap" in debug_taps:
            p.dma("act", lambda s, C_col=C_col: s.dma_start(
                out=c_tap.ap(), in_=C_col[:]), reads=[C_coln])

        s1m, s2m, csq = moms_s["y1"], moms_s["rs"], moms_s["y1sq"]
        p.comp("dve", lambda v: v.scalar_tensor_tensor(
            out=s1m[:], in0=moms["rs"][:], scalar=1.0, in1=C_col[:],
            op0=OP.bypass, op1=OP.mult),
            reads=["m_rs", C_coln], writes=["ms_y1"])
        p.comp("dve", lambda v: v.tensor_tensor(
            out=s1m[:], in0=s1m[:], in1=moms["y1"][:], op=OP.add),
            reads=["ms_y1", "m_y1"], writes=["ms_y1"])
        p.comp("dve", lambda v: v.tensor_tensor(
            out=csq[:], in0=C_col[:], in1=C_col[:], op=OP.mult),
            reads=[C_coln], writes=["ms_y1sq"])
        p.comp("dve", lambda v: v.tensor_tensor(
            out=s2m[:], in0=csq[:], in1=moms["rssq"][:], op=OP.mult),
            reads=["ms_y1sq", "m_rssq"], writes=["ms_rs"])
        p.comp("dve", lambda v: v.scalar_tensor_tensor(
            out=csq[:], in0=moms["y1rs"][:], scalar=2.0, in1=C_col[:],
            op0=OP.mult, op1=OP.mult),
            reads=["m_y1rs", C_coln], writes=["ms_y1sq"])
        p.comp("dve", lambda v: v.tensor_tensor(
            out=s2m[:], in0=s2m[:], in1=csq[:], op=OP.add),
            reads=["ms_rs", "ms_y1sq"], writes=["ms_rs"])
        p.comp("dve", lambda v: v.tensor_tensor(
            out=s2m[:], in0=s2m[:], in1=moms["y1sq"][:], op=OP.add),
            reads=["ms_rs", "m_y1sq"], writes=["ms_rs"])
        p.comp("dve", lambda v: v.reduce_sum(out=tinyA[:, 0:1], in_=s1m[:], axis=AX),
               reads=["ms_y1"], writes=["tinyA"])
        p.comp("dve", lambda v: v.reduce_sum(out=tinyA[:, 1:2], in_=s2m[:], axis=AX),
               reads=["ms_rs"], writes=["tinyA"])
        p.comp("pe", lambda t: t.matmul(out=pb[4][0:1, 0:2], lhsT=ones_col_f[:],
                                        rhs=tinyA[:, 0:2], start=True, stop=True),
               reads=["tinyA", "ones_col_f"], writes=["pb4"])
        p.comp("act", lambda s: s.copy(out=tinyB[0:1, 0:2], in_=pb[4][0:1, 0:2]),
               reads=["pb4"], writes=["tinyB"])
        p.dma("act", lambda s: s.dma_start(out=cc2_in.ap(), in_=tinyB[0:1, 0:2]),
              reads=["tinyB"], writes=["cc2_in"])
        p.cc(lambda g: g.collective_compute(
            "AllGather", OP.bypass, replica_groups=[list(range(NCORES))],
            ins=[cc2_in.ap()], outs=[cc2_out.ap()]),
            reads=["cc2_in"], writes=["cc2_out"])
        p.dma("pool", lambda s: s.dma_start(
            out=st_kr[:], in_=cc2_out.ap().rearrange("r k -> k r"),
            allow_slow_non_contiguous=True), reads=["cc2_out"], writes=["st_kr"])
        p.comp("dve", lambda v: v.tensor_tensor(
            out=st_kr[:], in0=st_kr[:], in1=bm_r[:], op=OP.mult),
            reads=["st_kr", "bm_r"], writes=["st_kr"])
        p.comp("dve", lambda v: v.reduce_sum(out=st_sc[:], in_=st_kr[:], axis=AX),
               reads=["st_kr"], writes=["st_sc"])
        NTD = 1.0 / (T * D)
        p.comp("dve", lambda v: v.tensor_scalar(
            out=st_sc[:], in0=st_sc[:], scalar1=NTD, scalar2=None, op0=OP.mult),
            reads=["st_sc"], writes=["st_sc"])
        p.dma("act", lambda s: s.dma_start(
            out=sc_bounce.ap().rearrange("o k -> k o"), in_=st_sc[:],
            allow_slow_non_contiguous=True), reads=["st_sc"], writes=["sc_bounce"])
        p.dma("pool", lambda s: s.dma_start(out=mq_row[:], in_=sc_bounce.ap()),
              reads=["sc_bounce"], writes=["mq_row"])
        p.comp("dve", lambda v: v.tensor_tensor(
            out=nvar_sc[:], in0=mq_row[0:1, 0:1], in1=mq_row[0:1, 0:1], op=OP.mult),
            reads=["mq_row"], writes=["nvar_sc"])
        p.comp("dve", lambda v: v.tensor_tensor(
            out=nvar_sc[:], in0=nvar_sc[:], in1=mq_row[0:1, 1:2], op=OP.subtract),
            reads=["nvar_sc", "mq_row"], writes=["nvar_sc"])
        p.comp("dve", lambda v: v.tensor_scalar(
            out=nvar_sc[:], in0=nvar_sc[:], scalar1=-1.0, scalar2=GN_EPS,
            op0=OP.mult, op1=OP.add),
            reads=["nvar_sc"], writes=["nvar_sc"])
        p.comp("act", lambda s: s.activation(out=nvar_sc[:], in_=nvar_sc[:], func=AF.Ln),
               reads=["nvar_sc"], writes=["nvar_sc"])
        p.comp("act", lambda s: s.activation(out=rstd_sc[:], in_=nvar_sc[:], func=AF.Exp,
                                             scale=-0.5),
               reads=["nvar_sc"], writes=["rstd_sc"])
        p.comp("pe", lambda t: t.matmul(out=pb[4][:, 0:1], lhsT=ones_row_f[:],
                                        rhs=mq_row[0:1, 0:1], start=True, stop=True),
               reads=["mq_row", "ones_row_f"], writes=["pb4"])
        p.comp("act", lambda s: s.copy(out=m_bc[:], in_=pb[4][:, 0:1]),
               reads=["pb4"], writes=["m_bc"])
        p.comp("pe", lambda t: t.matmul(out=pb[5][:, 0:1], lhsT=ones_row_f[:],
                                        rhs=rstd_sc[:], start=True, stop=True),
               reads=["rstd_sc", "ones_row_f"], writes=["pb5"])
        p.comp("act", lambda s: s.copy(out=r_bc[:], in_=pb[5][:, 0:1]),
               reads=["pb5"], writes=["r_bc"])
        if "stat_tap" in debug_taps:
            p.comp("dve", lambda v: v.tensor_copy(out=tinyB[0:1, 2:3], in_=m_bc[0:1, :]),
                   reads=["m_bc"], writes=["tinyB"])
            p.comp("dve", lambda v: v.tensor_copy(out=tinyB[0:1, 3:4], in_=r_bc[0:1, :]),
                   reads=["r_bc"], writes=["tinyB"])
            p.dma("act", lambda s: s.dma_start(out=stat_tap.ap(), in_=tinyB[0:1, 0:4]),
                  reads=["tinyB"])
        p.comp("dve", lambda v: v.tensor_scalar(
            out=a_col[:], in0=gnw_c[:], scalar1=r_bc[:], scalar2=None, op0=OP.mult),
            reads=["gnw_c", "r_bc"], writes=["a_col"])
        p.comp("dve", lambda v: v.scalar_tensor_tensor(
            out=nb_col[:], in0=a_col[:], scalar=m_bc[:], in1=gnb_c[:],
            op0=OP.mult, op1=OP.subtract),
            reads=["a_col", "m_bc", "gnb_c"], writes=["nb_col"])

        p.barrier()
        es2.close()
        es_mid.close()

        es3 = ExitStack()
        E3 = es3.enter_context
        y1_g = [E3(nc.sbuf_tensor(f"y1_g{i}", [128, 1024], BF16)) for i in range(2)]
        rs_g = [E3(nc.sbuf_tensor(f"rs_g{i}", [128, 1024], BF16)) for i in range(2)]
        nrm_su = [E3(nc.sbuf_tensor(f"nrm_su{i}", [128, NCH * TSUP], BF16)) for i in range(2)]
        o_t = E3(nc.sbuf_tensor("o_t", [128, 512], BF16))
        x1p = [E3(nc.sbuf_tensor(f"x1p{i}", [128, D], F32)) for i in range(2)]

        def p3_nrm(s_i, grps):
            spar = s_i % 2
            nsu, nsun = nrm_su[spar], f"nrm_su{spar}"
            for grp in grps:
                yb, ybn = y1_g[grp % 2], f"y1_g{grp % 2}"
                rb, rbn = rs_g[grp % 2], f"rs_g{grp % 2}"
                p.dma("sp", lambda s, s_i=s_i, grp=grp, yb=yb: s.dma_start(
                    out=yb[:], in_=y1_sp.ap()[s_i, grp]), reads=["y1_sp"], writes=[ybn])
                p.dma("sp", lambda s, s_i=s_i, grp=grp, rb=rb: s.dma_start(
                    out=rb[:], in_=rs_sp.ap()[s_i, grp]), reads=["rs_sp"], writes=[rbn])
                for gslot in range(2):
                    ec = grp * 2 + gslot
                    p.comp("dve", lambda v, ec=ec, gslot=gslot, yb=yb, rb=rb:
                           v.scalar_tensor_tensor(
                               out=o_t[:], in0=rb[:, gslot * 512:(gslot + 1) * 512],
                               scalar=C_col[:, ec:ec + 1],
                               in1=yb[:, gslot * 512:(gslot + 1) * 512],
                               op0=OP.mult, op1=OP.add),
                           reads=[rbn, ybn, C_coln], writes=["o_t"])
                    p.comp("dve", lambda v, ec=ec, nsu=nsu: v.tensor_scalar(
                        out=nsu[:, ec * TSUP:(ec + 1) * TSUP], in0=o_t[:],
                        scalar1=a_col[:, ec:ec + 1], scalar2=nb_col[:, ec:ec + 1],
                        op0=OP.mult, op1=OP.subtract),
                        reads=["o_t", "a_col", "nb_col"], writes=[nsun])

        def p3_A(u):
            s_i, j = divmod(u, NSUB)
            g = u * 128
            par = u % 2
            spar = s_i % 2
            nsu, nsun = nrm_su[spar], f"nrm_su{spar}"
            x1b, x1bn = x1p[par], f"x1p{par}"
            p.dma("sp", lambda s, g=g, x1b=x1b: s.dma_start(
                out=x1b[:], in_=x1_sp.ap()[g:g + 128, :]),
                reads=["x1_sp"], writes=[x1bn])
            for n in range(NNC):
                bank, bankn = pb[n % 3], f"pb{n % 3}"
                def mo_fn(t, j=j, n=n, bank=bank, nsu=nsu):
                    last = None
                    for ec in range(NCH):
                        last = t.matmul(
                            out=bank[:],
                            lhsT=nsu[:, ec * TSUP + j * 128: ec * TSUP + (j + 1) * 128],
                            rhs=woT[:, ec * D + n * 512: ec * D + (n + 1) * 512],
                            start=(ec == 0), stop=(ec == NCH - 1))
                    return last
                p.comp("pe", mo_fn, reads=[nsun, "qT"], writes=[bankn])
                p.comp("dve", lambda v, n=n, bank=bank, x1b=x1b: v.scalar_tensor_tensor(
                    out=x1b[:, n * 512:(n + 1) * 512], in0=bank[:], scalar=1.0,
                    in1=x1b[:, n * 512:(n + 1) * 512], op0=OP.bypass, op1=OP.add),
                    reads=[bankn, x1bn], writes=[x1bn])
            if "x2_tap" in debug_taps:
                p.dma("act", lambda s, g=g, x1b=x1b: s.dma_start(
                    out=x2_tap.ap()[g:g + 128, :], in_=x1b[:]), reads=[x1bn])

        def p3_B1(u):
            par = u % 2
            x1b, x1bn = x1p[par], f"x1p{par}"
            layer_norm(x1b, x1bn, 0)

        def p3_B2(u):
            s_i, j = divmod(u, NSUB)
            g = u * 128
            par = u % 2
            x1b, x1bn = x1p[par], f"x1p{par}"
            x3Tb, x3Tn = xn1T[par], f"xn1T{par}"
            transpose16(0, x3Tn,
                        lambda half, x3Tb=x3Tb: x3Tb[:, half * 1024:(half + 1) * 1024]
                        .rearrange("p (k w) -> p k w", w=128))
            for n in range(NNC):
                bank, bankn = pb[3 + n % 3], f"pb{3 + n % 3}"
                def mf_fn(t, n=n, bank=bank, x3Tb=x3Tb):
                    for kc in range(NCH):
                        t.matmul(out=bank[:], lhsT=x3Tb[:, kc * 128:(kc + 1) * 128],
                                 rhs=qT3[:, kc * D + n * 512: kc * D + (n + 1) * 512],
                                 start=(kc == 0), stop=False)
                    return t.matmul(out=bank[:], lhsT=ones_row[:],
                                    rhs=bias_bf[0:1, n * 512:(n + 1) * 512],
                                    start=False, stop=True)
                p.comp("pe", mf_fn, reads=[x3Tn, "wrT", "ones_row", "bias_bf"],
                       writes=[bankn])
                p.comp("dve", lambda v, n=n, bank=bank, x1b=x1b: v.scalar_tensor_tensor(
                    out=x1b[:, n * 512:(n + 1) * 512], in0=bank[:], scalar=0.5,
                    in1=x1b[:, n * 512:(n + 1) * 512], op0=OP.mult, op1=OP.add),
                    reads=[bankn, x1bn], writes=[x1bn])
            p.dma("act", lambda s, g=g, x1b=x1b: s.dma_start(
                out=out_ext[g:g + 128, :], in_=x1b[:]), reads=[x1bn], key="out")
            if "nrm_tap" in debug_taps and j == NSUB - 1:
                spar = s_i % 2
                p.dma("act", lambda s, s_i=s_i, spar=spar: s.dma_start(
                    out=nrm_tap.ap()[s_i], in_=nrm_su[spar][:]), reads=[f"nrm_su{spar}"])

        p3_nrm(0, list(range(NCH // 2)))
        p3_A(0)
        for u in range(NT):
            s_i, j = divmod(u, NSUB)
            if s_i + 1 < NSUP:
                p3_nrm(s_i + 1, list(range(j * 2, j * 2 + 2)))
            p3_B1(u)
            if u + 1 < NT:
                p3_A(u + 1)
            p3_B2(u)

        es3.close()
        block = E(nc.Block())
        p.finalize(block)

    return nc


def make_in_maps(inputs):
    x = np.asarray(inputs["L0_attn_x"], dtype=np.float32)
    wr = np.asarray(inputs["L0_attn_Wr"], dtype=np.float32)
    wk = np.asarray(inputs["L0_attn_Wk"], dtype=np.float32)
    wv = np.asarray(inputs["L0_attn_Wv"], dtype=np.float32)
    wo = np.asarray(inputs["L0_attn_Wo"], dtype=np.float32)
    decay = np.asarray(inputs["L0_attn_decay"], dtype=np.float32)
    gnw = np.asarray(inputs["L0_attn_gn_weight"], dtype=np.float32)
    gnb = np.asarray(inputs["L0_attn_gn_bias"], dtype=np.float32)
    wffn = np.asarray(inputs["L0_ffn_weight"], dtype=np.float32)
    bias = np.asarray(inputs["L0_ffn_bias"], dtype=np.float32)

    xr = np.ascontiguousarray(x.reshape(B, CPB, R, D))
    wrT = np.ascontiguousarray(wr.T)
    wkT = np.ascontiguousarray(wk.T)
    wvT = np.ascontiguousarray(wv.T)
    woT = np.ascontiguousarray(wo.T)
    gnw_c = np.ascontiguousarray(gnw.reshape(NCH, 128).T)
    gnb_c = np.ascontiguousarray(gnb.reshape(NCH, 128).T)
    decay_c = np.ascontiguousarray(decay.reshape(NCH, 128).T)
    bias_r = np.ascontiguousarray(bias.reshape(1, D))

    in_maps = []
    for c in range(NCORES):
        b, q = divmod(c, CPB)
        t0 = q * R
        pos = np.broadcast_to(
            (t0 + np.arange(R, dtype=np.float32)).reshape(1, R), (128, R))
        pm = np.zeros((128, NCORES), dtype=np.float32)
        for rk in range(b * CPB, b * CPB + q):
            pm[:, rk] = 1.0
        bm = np.zeros((2, NCORES), dtype=np.float32)
        bm[:, b * CPB:(b + 1) * CPB] = 1.0
        in_maps.append({
            "x": np.ascontiguousarray(xr[b, q]),
            "wrT": wrT, "wkT": wkT, "wvT": wvT, "woT": woT,
            "wffn": wffn, "bias": bias_r,
            "gnw": gnw_c, "gnb": gnb_c, "decay": decay_c,
            "pos": np.ascontiguousarray(pos), "pm": pm, "bm": bm,
        })
    return in_maps


def kernel(**inputs):
    nc = build_program()
    in_maps = make_in_maps(inputs)
    res = run_bass_kernel_spmd(nc, in_maps, core_ids=list(range(NCORES)))
    shards = [res.results[c]["out"] for c in range(NCORES)]
    out = np.stack(shards, axis=0).reshape(B, CPB, R, D).reshape(B, T, D)
    return np.ascontiguousarray(out.astype(np.float32))


# version 24
# speedup vs baseline: 1.0198x; 1.0198x over previous
import numpy as np
import ml_dtypes
from contextlib import ExitStack

import concourse.bass as bass
import concourse.mybir as mybir
from concourse.bass_utils import run_bass_kernel_spmd

F32 = mybir.dt.float32
BF16 = mybir.dt.bfloat16
OP = mybir.AluOpType
AF = mybir.ActivationFunctionType
AX = mybir.AxisListType.X

D = 2048
B = 2
T = 16384
NCORES = 8
CPB = 4
R = (B * T) // NCORES
NSUP = 8
TSUP = R // NSUP
NSUB = TSUP // 128
NT = NSUP * NSUB
NCH = D // 128
NNC = D // 512
INV_D = 1.0 / D
LN_EPS = 1e-5
GN_EPS = 1e-5
CLAMP_LO = -87.0
CLAMP_INV = 23.0258509299

ENGINES = ("pe", "act", "dve", "pool", "sp")
BLOCK_ATTR = {"pe": "tensor", "act": "scalar", "dve": "vector", "pool": "gpsimd", "sp": "sync"}


class Prog:

    def __init__(self, nc, stack):
        self.nc = nc
        self.stack = stack
        self.sem = {}
        for name in ("pe", "act", "dve", "pool", "sp", "cc"):
            self.sem[name] = stack.enter_context(nc.semaphore(name))
        self.cnt = {k: 0 for k in self.sem}
        self.actions = {e: [] for e in ENGINES}
        self.clock = {e: {} for e in ENGINES}
        self.res = {}
        self.nwaits = 0
        self.nsteps = 0

    def _dma_sem(self, key):
        name = "d_" + key
        if name not in self.sem:
            self.sem[name] = self.stack.enter_context(self.nc.semaphore(name))
            self.cnt[name] = 0
        return name

    def _record(self, eng, fn, sem_name, inc, reads, writes, kind):
        deps = []
        for r in reads:
            rec = self.res.get(r)
            if rec:
                deps += rec["w"]
        for w in writes:
            rec = self.res.get(w)
            if rec:
                deps += rec["w"] + rec["r"]
        clk = self.clock[eng]
        need = {}
        merge = []
        for (s, cval, snap) in deps:
            if kind == "compute" and s == eng and eng == "pe":
                continue
            if kind == "dma" and s == sem_name:
                continue
            if clk.get(s, 0) >= cval:
                continue
            if need.get(s, 0) < cval:
                need[s] = cval
            merge.append(snap)
        for snap in merge:
            for ks, kv in snap.items():
                if clk.get(ks, 0) < kv:
                    clk[ks] = kv
        for s, v in need.items():
            if clk.get(s, 0) < v:
                clk[s] = v
        waits = sorted(need.items())
        self.nwaits += len(waits)
        self.nsteps += 1

        self.cnt[sem_name] += inc
        count = self.cnt[sem_name]
        snap = dict(clk)
        snap[sem_name] = count
        evt = (sem_name, count, snap)
        for w in writes:
            self.res[w] = {"w": [evt], "r": []}
        for r in reads:
            rec = self.res.get(r)
            if rec is not None:
                rec["r"].append(evt)
        self.actions[eng].append((waits, fn, sem_name, inc))
        return evt

    def comp(self, eng, fn, reads=(), writes=()):
        return self._record(eng, fn, eng, 1, reads, writes, "compute")

    def dma(self, eng, fn, reads=(), writes=(), key=None):
        if key is None:
            key = writes[0] if writes else reads[0]
        return self._record(eng, fn, self._dma_sem(key), 16, reads, writes, "dma")

    def cc(self, fn, reads=(), writes=()):
        return self._record("pool", fn, "cc", 1, reads, writes, "cc")

    def barrier(self):
        for eng in ENGINES:
            clk = self.clock[eng]
            waits = []
            for s, c in self.cnt.items():
                if c > 0 and clk.get(s, 0) < c:
                    waits.append((s, c))
                    clk[s] = c
            if waits:
                self.actions[eng].append((sorted(waits), None, None, 0))

    def finalize(self, block):
        self.barrier()
        sems = self.sem
        for eng in ENGINES:
            attr = BLOCK_ATTR[eng]
            acts = self.actions[eng]

            def make_body(acts=acts):
                def body(section):
                    for waits, fn, sem_name, inc in acts:
                        for s, v in waits:
                            section.wait_ge(sems[s], v)
                        if fn is not None:
                            last = fn(section)
                            if inc:
                                last.then_inc(sems[sem_name], inc)
                return body

            getattr(block, attr)(make_body())


def build_program(debug_taps=()):
    nc = bass.Bass("TRN2", target_bir_lowering=False)

    x_in = nc.dram_tensor("x", [R, D], F32, kind="ExternalInput").ap()
    wrT_in = nc.dram_tensor("wrT", [D, D], F32, kind="ExternalInput").ap()
    wkT_in = nc.dram_tensor("wkT", [D, D], F32, kind="ExternalInput").ap()
    wvT_in = nc.dram_tensor("wvT", [D, D], F32, kind="ExternalInput").ap()
    woT_in = nc.dram_tensor("woT", [D, D], F32, kind="ExternalInput").ap()
    wffn_in = nc.dram_tensor("wffn", [D, D], F32, kind="ExternalInput").ap()
    bias_in = nc.dram_tensor("bias", [1, D], F32, kind="ExternalInput").ap()
    gnw_in = nc.dram_tensor("gnw", [128, NCH], F32, kind="ExternalInput").ap()
    gnb_in = nc.dram_tensor("gnb", [128, NCH], F32, kind="ExternalInput").ap()
    decay_in = nc.dram_tensor("decay", [128, NCH], F32, kind="ExternalInput").ap()
    pos_in = nc.dram_tensor("pos", [128, R], F32, kind="ExternalInput").ap()
    pm_in = nc.dram_tensor("pm", [128, NCORES], F32, kind="ExternalInput").ap()
    bm_in = nc.dram_tensor("bm", [2, NCORES], F32, kind="ExternalInput").ap()
    out_ext = nc.dram_tensor("out", [R, D], F32, kind="ExternalOutput").ap()

    def spill(name, shape, dtype):
        kind = "ExternalOutput" if name in debug_taps else "Internal"
        return nc.dram_tensor(name, shape, dtype, kind=kind)

    x1_sp = spill("x1_sp", [R, D], F32)
    xn2T_sp = spill("xn2T_sp", [NSUP, 128, NCH * TSUP], BF16)
    r_sp = spill("r_sp", [NSUP, 128, NCH * TSUP], BF16)
    y1_sp = spill("y1_sp", [NSUP, NCH // 2, 128, 1024], BF16)
    rs_sp = spill("rs_sp", [NSUP, NCH // 2, 128, 1024], BF16)
    qT_sp = spill("qT_sp", [128, NCH * D], BF16)
    tot_tap = spill("tot_tap", [128, NCH], F32)
    c_tap = spill("c_tap", [128, NCH], F32)
    stat_tap = spill("stat_tap", [1, 4], F32)
    nrm_tap = spill("nrm_tap", [NSUP, 128, NCH * TSUP], BF16)
    x2_tap = spill("x2_tap", [R, D], F32)
    wk_bf = nc.dram_tensor("wk_bf", [D, D], BF16)
    wv_bf = nc.dram_tensor("wv_bf", [D, D], BF16)
    wo_bf = nc.dram_tensor("wo_bf", [D, D], BF16)
    cc1_in = nc.dram_tensor("cc1_in", [128, NCH], F32)
    cc1_out = nc.dram_tensor("cc1_out", [NCORES * 128, NCH], F32)
    cc2_in = nc.dram_tensor("cc2_in", [1, 2], F32)
    cc2_out = nc.dram_tensor("cc2_out", [NCORES, 2], F32)
    sc_bounce = nc.dram_tensor("sc_bounce", [1, 2], F32)

    ident_d = nc.inline_tensor(np.eye(128, dtype=ml_dtypes.bfloat16), name="ident_c")

    with ExitStack() as stack:
        E = stack.enter_context
        p = Prog(nc, stack)

        ident = E(nc.sbuf_tensor("ident", [128, 128], BF16))
        ones_row = E(nc.sbuf_tensor("ones_row", [1, 128], BF16))
        bias_bf = E(nc.sbuf_tensor("bias_bf", [1, D], BF16))
        logld = E(nc.sbuf_tensor("logld", [128, NCH], F32))
        carry = E(nc.sbuf_tensor("carry", [128, NCH], F32))
        c_colA = E(nc.sbuf_tensor("c_colA", [128, NCH], F32))
        c_colB = E(nc.sbuf_tensor("c_colB", [128, NCH], F32))
        moms = {nm: E(nc.sbuf_tensor("m_" + nm, [128, NCH], F32))
                for nm in ("y1", "rs", "y1sq", "rssq", "y1rs")}
        moms_s = {nm: E(nc.sbuf_tensor("ms_" + nm, [128, NCH], F32))
                  for nm in ("y1", "rs", "y1sq", "rssq", "y1rs")}
        a_col = E(nc.sbuf_tensor("a_col", [128, NCH], F32))
        nb_col = E(nc.sbuf_tensor("nb_col", [128, NCH], F32))

        qT = E(nc.sbuf_tensor("qT", [128, NCH * D], BF16))
        wrT = E(nc.sbuf_tensor("wrT_sb", [128, NCH * D], BF16))
        xn_bf = [E(nc.sbuf_tensor(f"xn_bf{i}", [128, D], BF16)) for i in range(2)]
        xn1T = [E(nc.sbuf_tensor(f"xn1T{i}", [128, D], BF16)) for i in range(2)]
        s_col = [E(nc.sbuf_tensor(f"s_col{i}", [128, 1], F32)) for i in range(2)]
        q_col = [E(nc.sbuf_tensor(f"q_col{i}", [128, 1], F32)) for i in range(2)]
        mean_c = [E(nc.sbuf_tensor(f"mean_c{i}", [128, 1], F32)) for i in range(2)]
        rstd_c = [E(nc.sbuf_tensor(f"rstd_c{i}", [128, 1], F32)) for i in range(2)]

        pb = [E(nc.psum_tensor(f"pb{i}", [128, 512], F32)) for i in range(6)]
        pt = [E(nc.psum_tensor(f"pt{i}", [128, 1024], BF16)) for i in range(2)]

        es0 = ExitStack()
        E0 = es0.enter_context
        dtmp = E0(nc.sbuf_tensor("dtmp", [128, NCH], F32))
        wnat = [E0(nc.sbuf_tensor(f"wnat{i}", [128, D], F32)) for i in range(2)]
        gq_t = [E0(nc.sbuf_tensor(f"gq_t{i}", [128, D], BF16)) for i in range(2)]
        lq_t = [E0(nc.sbuf_tensor(f"lq_t{i}", [128, D], BF16)) for i in range(2)]
        qn_bf = [E0(nc.sbuf_tensor(f"qn_bf{i}", [128, D], BF16)) for i in range(2)]
        thr_c = E0(nc.sbuf_tensor("thr_c", [128, 1], F32))
        nthr_c = E0(nc.sbuf_tensor("nthr_c", [128, 1], F32))

        p.dma("sp", lambda s: s.dma_start(out=ident[:], in_=ident_d.ap()), writes=["ident"])
        p.dma("sp", lambda s: s.dma_start(out=logld[:], in_=decay_in), writes=["logld"])
        p.dma("pool", lambda g: g.dma_start(out=bias_bf[:], in_=bias_in), writes=["bias_bf"])
        p.comp("dve", lambda v: v.memset(ones_row[:], 1.0), writes=["ones_row"])
        p.comp("dve", lambda v: v.memset(carry[:], 0.0), writes=["carry"])
        for nm in moms:
            p.comp("dve", lambda v, nm=nm: v.memset(moms[nm][:], 0.0), writes=["m_" + nm])

        p.comp("act", lambda s: s.activation(out=dtmp[:], in_=logld[:], func=AF.Exp, scale=-1.0),
               reads=["logld"], writes=["dtmp"])
        p.comp("dve", lambda v: v.tensor_scalar(out=dtmp[:], in0=dtmp[:], scalar1=1.0,
                                                scalar2=None, op0=OP.add),
               reads=["dtmp"], writes=["dtmp"])
        p.comp("dve", lambda v: v.reciprocal(out=dtmp[:], in_=dtmp[:]),
               reads=["dtmp"], writes=["dtmp"])
        p.comp("dve", lambda v: v.tensor_scalar(out=dtmp[:], in0=dtmp[:], scalar1=1e-7,
                                                scalar2=None, op0=OP.max),
               reads=["dtmp"], writes=["dtmp"])
        p.comp("act", lambda s: s.activation(out=logld[:], in_=dtmp[:], func=AF.Ln),
               reads=["dtmp"], writes=["logld"])

        def q_load(oc):
            wb, wbn = wnat[oc % 2], f"wnat{oc % 2}"
            p.dma("sp", lambda s, oc=oc, wb=wb: s.dma_start(
                out=wb[:], in_=wffn_in[oc * 128:(oc + 1) * 128, :]), writes=[wbn])

        cast_jobs = []
        for kc in range(NCH):
            cast_jobs.append((lambda g, kc=kc: g.dma_start(
                out=wrT[:, kc * D:(kc + 1) * D],
                in_=wrT_in[kc * 128:(kc + 1) * 128, :]), "wrT"))
        NCC = 8
        for (dst, src, nm) in ((wk_bf, wkT_in, "wk_bf"), (wv_bf, wvT_in, "wv_bf"),
                               (wo_bf, woT_in, "wo_bf")):
            rows = D // NCC
            for i in range(NCC):
                cast_jobs.append((lambda g, dst=dst, src=src, i=i, rows=rows: g.dma_start(
                    out=dst.ap()[i * rows:(i + 1) * rows, :],
                    in_=src[i * rows:(i + 1) * rows, :]), nm))
        cast_i = 0

        def emit_casts(njobs):
            nonlocal cast_i
            for _ in range(njobs):
                if cast_i >= len(cast_jobs):
                    return
                fn, nm = cast_jobs[cast_i]
                p.dma("pool", fn, writes=[nm])
                cast_i += 1

        q_load(0)
        for oc in range(NCH):
            if oc + 1 < NCH:
                q_load(oc + 1)
            emit_casts(1)
            wb, wbn = wnat[oc % 2], f"wnat{oc % 2}"
            gq, gqn = gq_t[oc % 2], f"gq_t{oc % 2}"
            lq, lqn = lq_t[oc % 2], f"lq_t{oc % 2}"
            qn, qnn = qn_bf[oc % 2], f"qn_bf{oc % 2}"
            p.comp("dve", lambda v, wb=wb: v.reduce_sum(
                out=thr_c[:], in_=wb[:], axis=AX, apply_absolute_value=True),
                reads=[wbn], writes=["thr_c"])
            p.comp("dve", lambda v: v.tensor_scalar(
                out=thr_c[:], in0=thr_c[:], scalar1=0.7 * INV_D, scalar2=None, op0=OP.mult),
                reads=["thr_c"], writes=["thr_c"])
            p.comp("dve", lambda v: v.tensor_scalar(
                out=nthr_c[:], in0=thr_c[:], scalar1=-1.0, scalar2=None, op0=OP.mult),
                reads=["thr_c"], writes=["nthr_c"])
            p.comp("dve", lambda v, wb=wb, gq=gq: v.tensor_scalar(
                out=gq[:], in0=wb[:], scalar1=thr_c[:], scalar2=None, op0=OP.is_gt),
                reads=[wbn, "thr_c"], writes=[gqn])
            p.comp("dve", lambda v, wb=wb, lq=lq: v.tensor_scalar(
                out=lq[:], in0=wb[:], scalar1=nthr_c[:], scalar2=None, op0=OP.is_lt),
                reads=[wbn, "nthr_c"], writes=[lqn])
            p.comp("pool", lambda g, gq=gq, lq=lq, qn=qn: g.tensor_tensor(
                out=qn[:], in0=gq[:], in1=lq[:], op=OP.subtract),
                reads=[gqn, lqn], writes=[qnn])
            for half in range(2):
                def tr_fn(t, half=half, qn=qn):
                    last = None
                    for k in range(8):
                        kc = half * 8 + k
                        last = t.transpose(out=pt[half][:, k * 128:(k + 1) * 128],
                                           in_=qn[:, kc * 128:(kc + 1) * 128],
                                           identity=ident[:])
                    return last
                p.comp("pe", tr_fn, reads=[qnn, "ident"], writes=[f"pt{half}"])
                def ev_fn(s, oc=oc, half=half):
                    dst = qT.ap().rearrange("p (k w) -> p k w", w=D)[
                        :, half * 8:(half + 1) * 8, oc * 128:(oc + 1) * 128]
                    return s.copy(out=dst, in_=pt[half].ap().rearrange(
                        "p (k w) -> p k w", w=128))
                p.comp("act", ev_fn, reads=[f"pt{half}"], writes=["qT"])

        p.dma("act", lambda s: s.dma_start(out=qT_sp.ap(), in_=qT[:]),
              reads=["qT"], writes=["qT_sp"])


        p.barrier()
        es0.close()

        es_mid = ExitStack()
        xn2T = es_mid.enter_context(nc.sbuf_tensor("xn2T", [128, NCH * TSUP], BF16))
        r_bf = es_mid.enter_context(nc.sbuf_tensor("r_bf", [128, NCH * TSUP], BF16))

        def layer_norm(x_t, xname, par):
            xnb, sC, qC, mC, rC = xn_bf[par], s_col[par], q_col[par], mean_c[par], rstd_c[par]
            xnbn, sn, qn, mn, rn = (f"xn_bf{par}", f"s_col{par}", f"q_col{par}",
                                    f"mean_c{par}", f"rstd_c{par}")
            p.comp("dve", lambda v: v.reduce_sum(out=sC[:], in_=x_t[:], axis=AX),
                   reads=[xname], writes=[sn])
            p.comp("act", lambda s: s.activation(out=xnb[:], in_=x_t[:], func=AF.Square,
                                                 accum_out=qC[:]),
                   reads=[xname], writes=[xnbn, qn])
            p.comp("dve", lambda v: v.tensor_scalar(out=mC[:], in0=sC[:],
                                                    scalar1=INV_D, scalar2=None, op0=OP.mult),
                   reads=[sn], writes=[mn])
            p.comp("dve", lambda v: v.tensor_scalar(out=qC[:], in0=qC[:],
                                                    scalar1=INV_D, scalar2=None, op0=OP.mult),
                   reads=[qn], writes=[qn])
            p.comp("dve", lambda v: v.tensor_tensor(out=sC[:], in0=mC[:],
                                                    in1=mC[:], op=OP.mult),
                   reads=[mn], writes=[sn])
            p.comp("dve", lambda v: v.tensor_tensor(out=sC[:], in0=sC[:],
                                                    in1=qC[:], op=OP.subtract),
                   reads=[sn, qn], writes=[sn])
            p.comp("dve", lambda v: v.tensor_scalar(out=sC[:], in0=sC[:],
                                                    scalar1=-1.0, scalar2=LN_EPS,
                                                    op0=OP.mult, op1=OP.add),
                   reads=[sn], writes=[sn])
            p.comp("act", lambda s: s.activation(out=sC[:], in_=sC[:], func=AF.Ln),
                   reads=[sn], writes=[sn])
            p.comp("act", lambda s: s.activation(out=rC[:], in_=sC[:], func=AF.Exp,
                                                 scale=-0.5),
                   reads=[sn], writes=[rn])
            p.comp("dve", lambda v: v.tensor_scalar(out=xnb[:], in0=x_t[:],
                                                    scalar1=mC[:], scalar2=rC[:],
                                                    op0=OP.subtract, op1=OP.mult),
                   reads=[xname, mn, rn], writes=[xnbn])

        def transpose16(par, dstname, dst_view_fn):
            xnb, xnbn = xn_bf[par], f"xn_bf{par}"
            for half in range(2):
                def tr_fn(t, half=half, xnb=xnb):
                    last = None
                    for k in range(8):
                        kc = half * 8 + k
                        last = t.transpose(out=pt[half][:, k * 128:(k + 1) * 128],
                                           in_=xnb[:, kc * 128:(kc + 1) * 128],
                                           identity=ident[:])
                    return last
                p.comp("pe", tr_fn, reads=[xnbn, "ident"], writes=[f"pt{half}"])
                def ev_fn(s, half=half):
                    return s.copy(out=dst_view_fn(half), in_=pt[half].ap().rearrange(
                        "p (k w) -> p k w", w=128))
                p.comp("act", ev_fn, reads=[f"pt{half}"], writes=[dstname])

        es1 = ExitStack()
        E1 = es1.enter_context
        x_t = [E1(nc.sbuf_tensor(f"x_t{i}", [128, D], F32)) for i in range(3)]
        e_t = E1(nc.sbuf_tensor("e_t0", [128, 512], F32))

        def p1_A_pre(t):
            g = t * 128
            xb, xbn = x_t[t % 3], f"x_t{t % 3}"
            p.dma("sp", lambda s, g=g, xb=xb: s.dma_start(
                out=xb[:], in_=x_in[g:g + 128, :]), writes=[xbn])

        def p1_A_ln(t):
            xb, xbn = x_t[t % 3], f"x_t{t % 3}"
            layer_norm(xb, xbn, 0)

        def p1_A_tp(t):
            par = t % 2
            x1Tb, x1Tn = xn1T[par], f"xn1T{par}"
            transpose16(0, x1Tn,
                        lambda half, x1Tb=x1Tb: x1Tb[:, half * 1024:(half + 1) * 1024]
                        .rearrange("p (k w) -> p k w", w=128))

        def p1_B1a(t):
            g = t * 128
            par = t % 2
            xb, xbn = x_t[t % 3], f"x_t{t % 3}"
            x1Tb, x1Tn = xn1T[par], f"xn1T{par}"
            for n in range(NNC):
                bank, bankn = pb[n], f"pb{n}"
                def mm_fn(tt, n=n, bank=bank, x1Tb=x1Tb):
                    for kc in range(NCH):
                        tt.matmul(out=bank[:], lhsT=x1Tb[:, kc * 128:(kc + 1) * 128],
                                  rhs=qT[:, kc * D + n * 512: kc * D + (n + 1) * 512],
                                  start=(kc == 0), stop=False)
                    return tt.matmul(out=bank[:], lhsT=ones_row[:],
                                     rhs=bias_bf[0:1, n * 512:(n + 1) * 512],
                                     start=False, stop=True)
                p.comp("pe", mm_fn, reads=[x1Tn, "qT", "ones_row", "bias_bf"],
                       writes=[bankn])
                p.comp("dve", lambda v, n=n, bank=bank, xb=xb: v.scalar_tensor_tensor(
                    out=xb[:, n * 512:(n + 1) * 512], in0=bank[:],
                    scalar=0.5, in1=xb[:, n * 512:(n + 1) * 512],
                    op0=OP.mult, op1=OP.add),
                    reads=[bankn, xbn], writes=[xbn])
            p.dma("act", lambda s, g=g, xb=xb: s.dma_start(
                out=x1_sp.ap()[g:g + 128, :], in_=xb[:]),
                reads=[xbn], writes=[f"x1_sp{t % 3}"])

        def p1_B1b(t):
            xb, xbn = x_t[t % 3], f"x_t{t % 3}"
            layer_norm(xb, xbn, 1)

        def p1_B2(t):
            j = t % NSUB
            transpose16(1, "xn2T",
                        lambda half, j=j: xn2T.ap().rearrange(
                            "p (k w) -> p k w", w=TSUP)[
                            :, half * 8:(half + 1) * 8, j * 128:(j + 1) * 128])

        def p1_R(s_i):
            p.dma("act", lambda s, s_i=s_i: s.dma_start(
                out=xn2T_sp.ap()[s_i], in_=xn2T[:]), reads=["xn2T"], writes=["xn2T_sp"])
            for ec in range(NCH):
                bank, bankn = pb[4 + ec % 2], f"pb{4 + ec % 2}"
                def mmr_fn(t, ec=ec, bank=bank):
                    last = None
                    for kc in range(NCH):
                        last = t.matmul(out=bank[:],
                                        lhsT=wrT[:, kc * D + ec * 128: kc * D + (ec + 1) * 128],
                                        rhs=xn2T[:, kc * TSUP:(kc + 1) * TSUP],
                                        start=(kc == 0), stop=(kc == NCH - 1))
                    return last
                p.comp("pe", mmr_fn, reads=["wrT", "xn2T"], writes=[bankn])
                p.comp("act", lambda s, bank=bank: s.activation(
                    out=e_t[:], in_=bank[:], func=AF.Sigmoid),
                    reads=[bankn], writes=["e_t0"])
                p.comp("pool", lambda g, ec=ec: g.tensor_copy(
                    out=r_bf[:, ec * TSUP:(ec + 1) * TSUP], in_=e_t[:]),
                    reads=["e_t0"], writes=["r_bf"])
            p.dma("act", lambda s, s_i=s_i: s.dma_start(
                out=r_sp.ap()[s_i], in_=r_bf[:]), reads=["r_bf"], writes=["r_sp"])

        p1_A_pre(0)
        p1_A_pre(1)
        p1_A_ln(0)
        p1_A_tp(0)
        p1_A_ln(1)
        pending_b2 = None
        for t in range(NT):
            if t + 2 < NT:
                p1_A_pre(t + 2)
            p1_B1a(t)
            emit_casts(1)
            if t + 1 < NT:
                p1_A_tp(t + 1)
            if pending_b2 is not None:
                p1_B2(pending_b2)
                pending_b2 = None
            p1_B1b(t)
            if t + 2 < NT:
                p1_A_ln(t + 2)
            if t % NSUB == NSUB - 1:
                p1_B2(t)
                p1_R(t // NSUB)
            else:
                pending_b2 = t

        wkT = qT
        wvT = wrT
        p.dma("sp", lambda s: s.dma_start(
            out=wkT.ap().rearrange("p (k w) -> p k w", w=D),
            in_=wk_bf.ap().rearrange("(k p) w -> p k w", p=128)),
            reads=["wk_bf"], writes=["qT"], key="qT_w")
        p.dma("sp", lambda s: s.dma_start(
            out=wvT.ap().rearrange("p (k w) -> p k w", w=D),
            in_=wv_bf.ap().rearrange("(k p) w -> p k w", p=128)),
            reads=["wv_bf"], writes=["wrT"], key="wrT_w")

        p.barrier()
        es1.close()

        es2 = ExitStack()
        E2 = es2.enter_context
        pos_t = E2(nc.sbuf_tensor("pos_t0", [128, TSUP], F32))
        ones_t = E2(nc.sbuf_tensor("ones_t", [128, 512], BF16))
        p.comp("dve", lambda v: v.memset(ones_t[:], 1.0), writes=["ones_t"])
        v_t = [E2(nc.sbuf_tensor(f"v_t{i}", [128, 512], F32)) for i in range(2)]
        a_t = [E2(nc.sbuf_tensor(f"a_t{i}", [128, 512], F32)) for i in range(2)]
        scale_t = [E2(nc.sbuf_tensor(f"scale_t{i}", [128, 512], F32)) for i in range(2)]
        kv_t = [E2(nc.sbuf_tensor(f"kv_t{i}", [128, 512], F32)) for i in range(2)]
        lc_t = E2(nc.sbuf_tensor("lc_t", [128, 512], F32))
        sc_t = E2(nc.sbuf_tensor("sc_t", [128, 512], F32))
        y1_grp = E2(nc.sbuf_tensor("y1_grp", [128, 512], BF16))
        rs_grp = E2(nc.sbuf_tensor("rs_grp", [128, 512], BF16))

        def p2_head(s_i):
            p.dma("sp", lambda s, s_i=s_i: s.dma_start(
                out=xn2T[:], in_=xn2T_sp.ap()[s_i]), reads=["xn2T_sp"], writes=["xn2T"])
            p.dma("sp", lambda s, s_i=s_i: s.dma_start(
                out=r_bf[:], in_=r_sp.ap()[s_i]), reads=["r_sp"], writes=["r_bf"])
            p.dma("sp", lambda s, s_i=s_i: s.dma_start(
                out=pos_t[:], in_=pos_in[:, s_i * TSUP:(s_i + 1) * TSUP]),
                writes=["pos_t0"])
            for nm in moms_s:
                p.comp("dve", lambda v, nm=nm: v.memset(moms_s[nm][:], 0.0),
                       writes=["ms_" + nm])

        def p2_A(s_i, ec):
            kbank, kbankn = pb[ec % 3], f"pb{ec % 3}"
            vbank, vbankn = pb[3 + ec % 3], f"pb{3 + ec % 3}"
            def mmk_fn(t, ec=ec, bank=kbank):
                last = None
                for kc in range(NCH):
                    last = t.matmul(out=bank[:],
                                    lhsT=wkT[:, kc * D + ec * 128: kc * D + (ec + 1) * 128],
                                    rhs=xn2T[:, kc * TSUP:(kc + 1) * TSUP],
                                    start=(kc == 0), stop=(kc == NCH - 1))
                return last
            p.comp("pe", mmk_fn, reads=["qT", "xn2T"], writes=[kbankn])
            def mmv_fn(t, ec=ec, bank=vbank):
                last = None
                for kc in range(NCH):
                    last = t.matmul(out=bank[:],
                                    lhsT=wvT[:, kc * D + ec * 128: kc * D + (ec + 1) * 128],
                                    rhs=xn2T[:, kc * TSUP:(kc + 1) * TSUP],
                                    start=(kc == 0), stop=(kc == NCH - 1))
                return last
            p.comp("pe", mmv_fn, reads=["wrT", "xn2T"], writes=[vbankn])

        def p2_B_front(s_i, ec):
            kbank, kbankn = pb[ec % 3], f"pb{ec % 3}"
            aP, aPn = a_t[ec % 2], f"a_t{ec % 2}"
            scP, scPn = scale_t[ec % 2], f"scale_t{ec % 2}"
            kvP, kvPn = kv_t[ec % 2], f"kv_t{ec % 2}"
            vbank, vbankn = pb[3 + ec % 3], f"pb{3 + ec % 3}"
            p.comp("act", lambda s, bank=vbank, ec=ec: s.activation(
                out=v_t[ec % 2][:], in_=bank[:], func=AF.Copy),
                reads=[vbankn], writes=[f"v_t{ec % 2}"])
            p.comp("dve", lambda v, ec=ec, aP=aP: v.tensor_scalar(
                out=aP[:], in0=pos_t[:], scalar1=logld[:, ec:ec + 1],
                scalar2=CLAMP_LO, op0=OP.mult, op1=OP.max),
                reads=["pos_t0", "logld"], writes=[aPn])
            p.comp("act", lambda s, aP=aP, scP=scP: s.activation(
                out=scP[:], in_=aP[:], func=AF.Exp),
                reads=[aPn], writes=[scPn])
            p.comp("dve", lambda v, ec=ec, aP=aP: v.tensor_scalar(
                out=aP[:], in0=pos_t[:], scalar1=logld[:, ec:ec + 1],
                scalar2=-CLAMP_INV, op0=OP.mult, op1=OP.max),
                reads=["pos_t0", "logld"], writes=[aPn])
            p.comp("act", lambda s, aP=aP: s.activation(
                out=aP[:], in_=aP[:], func=AF.Exp, scale=-1.0),
                reads=[aPn], writes=[aPn])
            p.comp("dve", lambda v, bank=kbank, kvP=kvP, ec=ec: v.tensor_tensor(
                out=kvP[:], in0=bank[:], in1=v_t[ec % 2][:], op=OP.mult),
                reads=[kbankn, f"v_t{ec % 2}"], writes=[kvPn])

        def p2_B_mid(s_i, ec):
            aP, aPn = a_t[ec % 2], f"a_t{ec % 2}"
            kvP, kvPn = kv_t[ec % 2], f"kv_t{ec % 2}"
            p.comp("pool", lambda g, kvP=kvP, aP=aP: g.tensor_tensor(
                out=sc_t[:], in0=kvP[:], in1=aP[:], op=OP.mult),
                reads=[kvPn, aPn], writes=["sc_t"])
            p.comp("dve", lambda v, ec=ec: v.tensor_tensor_scan(
                out=lc_t[:], data0=ones_t[:], data1=sc_t[:],
                initial=carry[:, ec:ec + 1], op0=OP.mult, op1=OP.add),
                reads=["sc_t", "carry", "ones_t"], writes=["lc_t"])
            p.comp("dve", lambda v, ec=ec: v.tensor_copy(
                out=carry[:, ec:ec + 1], in_=lc_t[:, 511:512]),
                reads=["lc_t"], writes=["carry"])

        def p2_B_tail(s_i, ec):
            grp, gslot = ec // 2, ec % 2
            scP, scPn = scale_t[ec % 2], f"scale_t{ec % 2}"
            kvP, kvPn = kv_t[ec % 2], f"kv_t{ec % 2}"
            p.comp("pool", lambda g, kvP=kvP, scP=scP: g.tensor_tensor(
                out=kvP[:], in0=lc_t[:], in1=scP[:], op=OP.mult),
                reads=["lc_t", scPn], writes=[kvPn])
            p.comp("dve", lambda v, ec=ec, kvP=kvP: v.scalar_tensor_tensor(
                out=y1_grp[:], in0=kvP[:],
                scalar=1.0, in1=r_bf[:, ec * TSUP:(ec + 1) * TSUP],
                op0=OP.bypass, op1=OP.mult, accum_out=moms_s["y1"][:, ec:ec + 1]),
                reads=[kvPn, "r_bf"], writes=["y1_grp", "ms_y1"])
            p.comp("dve", lambda v, ec=ec, scP=scP: v.scalar_tensor_tensor(
                out=rs_grp[:], in0=scP[:],
                scalar=1.0, in1=r_bf[:, ec * TSUP:(ec + 1) * TSUP],
                op0=OP.bypass, op1=OP.mult, accum_out=moms_s["rs"][:, ec:ec + 1]),
                reads=[scPn, "r_bf"], writes=["rs_grp", "ms_rs"])
            dmpb, dmpn = a_t[ec % 2], f"a_t{ec % 2}"
            p.comp("dve", lambda v, ec=ec, dmpb=dmpb: v.scalar_tensor_tensor(
                out=dmpb[:], in0=y1_grp[:],
                scalar=1.0, in1=rs_grp[:],
                op0=OP.bypass, op1=OP.mult, accum_out=moms_s["y1rs"][:, ec:ec + 1]),
                reads=["y1_grp", "rs_grp"], writes=[dmpn, "ms_y1rs"])
            p.comp("act", lambda s, ec=ec, dmpb=dmpb: s.activation(
                out=dmpb[:], in_=y1_grp[:],
                func=AF.Square, accum_out=moms_s["y1sq"][:, ec:ec + 1]),
                reads=["y1_grp"], writes=[dmpn, "ms_y1sq"])
            p.comp("act", lambda s, ec=ec, dmpb=dmpb: s.activation(
                out=dmpb[:], in_=rs_grp[:],
                func=AF.Square, accum_out=moms_s["rssq"][:, ec:ec + 1]),
                reads=["rs_grp"], writes=[dmpn, "ms_rssq"])
            p.dma("act", lambda s, s_i=s_i, grp=grp, gslot=gslot: s.dma_start(
                out=y1_sp.ap()[s_i, grp][:, gslot * 512:(gslot + 1) * 512],
                in_=y1_grp[:]),
                reads=["y1_grp"], writes=["y1_sp"])
            p.dma("act", lambda s, s_i=s_i, grp=grp, gslot=gslot: s.dma_start(
                out=rs_sp.ap()[s_i, grp][:, gslot * 512:(gslot + 1) * 512],
                in_=rs_grp[:]),
                reads=["rs_grp"], writes=["rs_sp"])
            if ec == NCH - 1:
                for nm in moms:
                    p.comp("dve", lambda v, nm=nm: v.tensor_tensor(
                        out=moms[nm][:], in0=moms[nm][:], in1=moms_s[nm][:], op=OP.add),
                        reads=["m_" + nm, "ms_" + nm], writes=["m_" + nm])

        for s_i in range(NSUP):
            p2_head(s_i)
            p2_A(s_i, 0)
            p2_A(s_i, 1)
            p2_B_front(s_i, 0)
            for ec in range(NCH):
                if ec + 2 < NCH:
                    p2_A(s_i, ec + 2)
                p2_B_mid(s_i, ec)
                if ec + 1 < NCH:
                    p2_B_front(s_i, ec + 1)
                p2_B_tail(s_i, ec)

        woT = qT
        qT3 = wrT
        p.dma("sp", lambda s: s.dma_start(
            out=woT.ap().rearrange("p (k w) -> p k w", w=D),
            in_=wo_bf.ap().rearrange("(k p) w -> p k w", p=128)),
            reads=["wo_bf"], writes=["qT"], key="qT_w")
        p.dma("sp", lambda s: s.dma_start(out=qT3[:], in_=qT_sp.ap()),
              reads=["qT_sp"], writes=["wrT"], key="wrT_w")

        gnw_c = E2(nc.sbuf_tensor("gnw_c", [128, NCH], F32))
        gnb_c = E2(nc.sbuf_tensor("gnb_c", [128, NCH], F32))
        pm_bc = E2(nc.sbuf_tensor("pm_bc", [128, NCORES], F32))
        bm_r = E2(nc.sbuf_tensor("bm_r", [2, NCORES], F32))
        ones_col_f = E2(nc.sbuf_tensor("ones_col_f", [128, 1], F32))
        ones_row_f = E2(nc.sbuf_tensor("ones_row_f", [1, 128], F32))
        tinyA = E2(nc.sbuf_tensor("tinyA", [128, 8], F32))
        tinyB = E2(nc.sbuf_tensor("tinyB", [128, 8], F32))
        m_bc = E2(nc.sbuf_tensor("m_bc", [128, 1], F32))
        r_bc = E2(nc.sbuf_tensor("r_bc", [128, 1], F32))
        tot_r2 = (moms_s["rssq"], moms_s["y1rs"])
        tot_r2n = ("ms_rssq", "ms_y1rs")
        st_kr = E2(nc.sbuf_tensor("st_kr", [2, NCORES], F32))
        st_sc = E2(nc.sbuf_tensor("st_sc", [2, 1], F32))
        mq_row = E2(nc.sbuf_tensor("mq_row", [1, 2], F32))
        nvar_sc = E2(nc.sbuf_tensor("nvar_sc", [1, 1], F32))
        rstd_sc = E2(nc.sbuf_tensor("rstd_sc", [1, 1], F32))
        p.dma("pool", lambda s: s.dma_start(out=gnw_c[:], in_=gnw_in), writes=["gnw_c"])
        p.dma("pool", lambda s: s.dma_start(out=gnb_c[:], in_=gnb_in), writes=["gnb_c"])
        p.dma("pool", lambda s: s.dma_start(out=pm_bc[:], in_=pm_in), writes=["pm_bc"])
        p.dma("pool", lambda s: s.dma_start(out=bm_r[:], in_=bm_in), writes=["bm_r"])
        p.comp("dve", lambda v: v.memset(ones_col_f[:], 1.0), writes=["ones_col_f"])
        p.comp("dve", lambda v: v.memset(ones_row_f[:], 1.0), writes=["ones_row_f"])
        p.dma("act", lambda s: s.dma_start(out=cc1_in.ap(), in_=carry[:]),
              reads=["carry"], writes=["cc1_in"])
        if "tot_tap" in debug_taps:
            p.dma("act", lambda s: s.dma_start(out=tot_tap.ap(), in_=carry[:]),
                  reads=["carry"])
        p.cc(lambda g: g.collective_compute(
            "AllGather", OP.bypass, replica_groups=[list(range(NCORES))],
            ins=[cc1_in.ap()], outs=[cc1_out.ap()]),
            reads=["cc1_in"], writes=["cc1_out"])
        p.comp("dve", lambda v: v.memset(c_colA[:], 0.0), writes=["c_colA"])
        cur, curn = c_colA, "c_colA"
        for rank in range(NCORES):
            trb, trn = tot_r2[rank % 2], tot_r2n[rank % 2]
            p.dma("pool", lambda s, rank=rank, trb=trb: s.dma_start(
                out=trb[:], in_=cc1_out.ap()[rank * 128:(rank + 1) * 128, :]),
                reads=["cc1_out"], writes=[trn])
            nxt, nxtn = ((c_colB, "c_colB") if cur is c_colA else (c_colA, "c_colA"))
            p.comp("dve", lambda v, rank=rank, cur=cur, nxt=nxt, trb=trb:
                   v.scalar_tensor_tensor(
                out=nxt[:], in0=trb[:], scalar=pm_bc[:, rank:rank + 1], in1=cur[:],
                op0=OP.mult, op1=OP.add),
                reads=[trn, "pm_bc", curn], writes=[nxtn])
            cur, curn = nxt, nxtn
        C_col, C_coln = cur, curn
        if "c_tap" in debug_taps:
            p.dma("act", lambda s, C_col=C_col: s.dma_start(
                out=c_tap.ap(), in_=C_col[:]), reads=[C_coln])

        s1m, s2m, csq = moms_s["y1"], moms_s["rs"], moms_s["y1sq"]
        p.comp("dve", lambda v: v.scalar_tensor_tensor(
            out=s1m[:], in0=moms["rs"][:], scalar=1.0, in1=C_col[:],
            op0=OP.bypass, op1=OP.mult),
            reads=["m_rs", C_coln], writes=["ms_y1"])
        p.comp("dve", lambda v: v.tensor_tensor(
            out=s1m[:], in0=s1m[:], in1=moms["y1"][:], op=OP.add),
            reads=["ms_y1", "m_y1"], writes=["ms_y1"])
        p.comp("dve", lambda v: v.tensor_tensor(
            out=csq[:], in0=C_col[:], in1=C_col[:], op=OP.mult),
            reads=[C_coln], writes=["ms_y1sq"])
        p.comp("dve", lambda v: v.tensor_tensor(
            out=s2m[:], in0=csq[:], in1=moms["rssq"][:], op=OP.mult),
            reads=["ms_y1sq", "m_rssq"], writes=["ms_rs"])
        p.comp("dve", lambda v: v.scalar_tensor_tensor(
            out=csq[:], in0=moms["y1rs"][:], scalar=2.0, in1=C_col[:],
            op0=OP.mult, op1=OP.mult),
            reads=["m_y1rs", C_coln], writes=["ms_y1sq"])
        p.comp("dve", lambda v: v.tensor_tensor(
            out=s2m[:], in0=s2m[:], in1=csq[:], op=OP.add),
            reads=["ms_rs", "ms_y1sq"], writes=["ms_rs"])
        p.comp("dve", lambda v: v.tensor_tensor(
            out=s2m[:], in0=s2m[:], in1=moms["y1sq"][:], op=OP.add),
            reads=["ms_rs", "m_y1sq"], writes=["ms_rs"])
        p.comp("dve", lambda v: v.reduce_sum(out=tinyA[:, 0:1], in_=s1m[:], axis=AX),
               reads=["ms_y1"], writes=["tinyA"])
        p.comp("dve", lambda v: v.reduce_sum(out=tinyA[:, 1:2], in_=s2m[:], axis=AX),
               reads=["ms_rs"], writes=["tinyA"])
        p.comp("pe", lambda t: t.matmul(out=pb[4][0:1, 0:2], lhsT=ones_col_f[:],
                                        rhs=tinyA[:, 0:2], start=True, stop=True),
               reads=["tinyA", "ones_col_f"], writes=["pb4"])
        p.comp("act", lambda s: s.copy(out=tinyB[0:1, 0:2], in_=pb[4][0:1, 0:2]),
               reads=["pb4"], writes=["tinyB"])
        p.dma("act", lambda s: s.dma_start(out=cc2_in.ap(), in_=tinyB[0:1, 0:2]),
              reads=["tinyB"], writes=["cc2_in"])
        p.cc(lambda g: g.collective_compute(
            "AllGather", OP.bypass, replica_groups=[list(range(NCORES))],
            ins=[cc2_in.ap()], outs=[cc2_out.ap()]),
            reads=["cc2_in"], writes=["cc2_out"])
        p.dma("pool", lambda s: s.dma_start(
            out=st_kr[:], in_=cc2_out.ap().rearrange("r k -> k r"),
            allow_slow_non_contiguous=True), reads=["cc2_out"], writes=["st_kr"])
        p.comp("dve", lambda v: v.tensor_tensor(
            out=st_kr[:], in0=st_kr[:], in1=bm_r[:], op=OP.mult),
            reads=["st_kr", "bm_r"], writes=["st_kr"])
        p.comp("dve", lambda v: v.reduce_sum(out=st_sc[:], in_=st_kr[:], axis=AX),
               reads=["st_kr"], writes=["st_sc"])
        NTD = 1.0 / (T * D)
        p.comp("dve", lambda v: v.tensor_scalar(
            out=st_sc[:], in0=st_sc[:], scalar1=NTD, scalar2=None, op0=OP.mult),
            reads=["st_sc"], writes=["st_sc"])
        p.dma("act", lambda s: s.dma_start(
            out=sc_bounce.ap().rearrange("o k -> k o"), in_=st_sc[:],
            allow_slow_non_contiguous=True), reads=["st_sc"], writes=["sc_bounce"])
        p.dma("pool", lambda s: s.dma_start(out=mq_row[:], in_=sc_bounce.ap()),
              reads=["sc_bounce"], writes=["mq_row"])
        p.comp("dve", lambda v: v.tensor_tensor(
            out=nvar_sc[:], in0=mq_row[0:1, 0:1], in1=mq_row[0:1, 0:1], op=OP.mult),
            reads=["mq_row"], writes=["nvar_sc"])
        p.comp("dve", lambda v: v.tensor_tensor(
            out=nvar_sc[:], in0=nvar_sc[:], in1=mq_row[0:1, 1:2], op=OP.subtract),
            reads=["nvar_sc", "mq_row"], writes=["nvar_sc"])
        p.comp("dve", lambda v: v.tensor_scalar(
            out=nvar_sc[:], in0=nvar_sc[:], scalar1=-1.0, scalar2=GN_EPS,
            op0=OP.mult, op1=OP.add),
            reads=["nvar_sc"], writes=["nvar_sc"])
        p.comp("act", lambda s: s.activation(out=nvar_sc[:], in_=nvar_sc[:], func=AF.Ln),
               reads=["nvar_sc"], writes=["nvar_sc"])
        p.comp("act", lambda s: s.activation(out=rstd_sc[:], in_=nvar_sc[:], func=AF.Exp,
                                             scale=-0.5),
               reads=["nvar_sc"], writes=["rstd_sc"])
        p.comp("pe", lambda t: t.matmul(out=pb[4][:, 0:1], lhsT=ones_row_f[:],
                                        rhs=mq_row[0:1, 0:1], start=True, stop=True),
               reads=["mq_row", "ones_row_f"], writes=["pb4"])
        p.comp("act", lambda s: s.copy(out=m_bc[:], in_=pb[4][:, 0:1]),
               reads=["pb4"], writes=["m_bc"])
        p.comp("pe", lambda t: t.matmul(out=pb[5][:, 0:1], lhsT=ones_row_f[:],
                                        rhs=rstd_sc[:], start=True, stop=True),
               reads=["rstd_sc", "ones_row_f"], writes=["pb5"])
        p.comp("act", lambda s: s.copy(out=r_bc[:], in_=pb[5][:, 0:1]),
               reads=["pb5"], writes=["r_bc"])
        if "stat_tap" in debug_taps:
            p.comp("dve", lambda v: v.tensor_copy(out=tinyB[0:1, 2:3], in_=m_bc[0:1, :]),
                   reads=["m_bc"], writes=["tinyB"])
            p.comp("dve", lambda v: v.tensor_copy(out=tinyB[0:1, 3:4], in_=r_bc[0:1, :]),
                   reads=["r_bc"], writes=["tinyB"])
            p.dma("act", lambda s: s.dma_start(out=stat_tap.ap(), in_=tinyB[0:1, 0:4]),
                  reads=["tinyB"])
        p.comp("dve", lambda v: v.tensor_scalar(
            out=a_col[:], in0=gnw_c[:], scalar1=r_bc[:], scalar2=None, op0=OP.mult),
            reads=["gnw_c", "r_bc"], writes=["a_col"])
        p.comp("dve", lambda v: v.scalar_tensor_tensor(
            out=nb_col[:], in0=a_col[:], scalar=m_bc[:], in1=gnb_c[:],
            op0=OP.mult, op1=OP.subtract),
            reads=["a_col", "m_bc", "gnb_c"], writes=["nb_col"])

        p.barrier()
        es2.close()
        es_mid.close()

        es3 = ExitStack()
        E3 = es3.enter_context
        y1_g = [E3(nc.sbuf_tensor(f"y1_g{i}", [128, 1024], BF16)) for i in range(2)]
        rs_g = [E3(nc.sbuf_tensor(f"rs_g{i}", [128, 1024], BF16)) for i in range(2)]
        nrm_su = [E3(nc.sbuf_tensor(f"nrm_su{i}", [128, NCH * TSUP], BF16)) for i in range(2)]
        o_t = E3(nc.sbuf_tensor("o_t", [128, 512], BF16))
        x1p = [E3(nc.sbuf_tensor(f"x1p{i}", [128, D], F32)) for i in range(2)]

        def p3_nrm(s_i, grps):
            spar = s_i % 2
            nsu, nsun = nrm_su[spar], f"nrm_su{spar}"
            for grp in grps:
                yb, ybn = y1_g[grp % 2], f"y1_g{grp % 2}"
                rb, rbn = rs_g[grp % 2], f"rs_g{grp % 2}"
                p.dma("sp", lambda s, s_i=s_i, grp=grp, yb=yb: s.dma_start(
                    out=yb[:], in_=y1_sp.ap()[s_i, grp]), reads=["y1_sp"], writes=[ybn])
                p.dma("sp", lambda s, s_i=s_i, grp=grp, rb=rb: s.dma_start(
                    out=rb[:], in_=rs_sp.ap()[s_i, grp]), reads=["rs_sp"], writes=[rbn])
                for gslot in range(2):
                    ec = grp * 2 + gslot
                    p.comp("dve", lambda v, ec=ec, gslot=gslot, yb=yb, rb=rb:
                           v.scalar_tensor_tensor(
                               out=o_t[:], in0=rb[:, gslot * 512:(gslot + 1) * 512],
                               scalar=C_col[:, ec:ec + 1],
                               in1=yb[:, gslot * 512:(gslot + 1) * 512],
                               op0=OP.mult, op1=OP.add),
                           reads=[rbn, ybn, C_coln], writes=["o_t"])
                    p.comp("dve", lambda v, ec=ec, nsu=nsu: v.tensor_scalar(
                        out=nsu[:, ec * TSUP:(ec + 1) * TSUP], in0=o_t[:],
                        scalar1=a_col[:, ec:ec + 1], scalar2=nb_col[:, ec:ec + 1],
                        op0=OP.mult, op1=OP.subtract),
                        reads=["o_t", "a_col", "nb_col"], writes=[nsun])

        def p3_A(u):
            s_i, j = divmod(u, NSUB)
            g = u * 128
            par = u % 2
            spar = s_i % 2
            nsu, nsun = nrm_su[spar], f"nrm_su{spar}"
            x1b, x1bn = x1p[par], f"x1p{par}"
            p.dma("sp", lambda s, g=g, x1b=x1b: s.dma_start(
                out=x1b[:], in_=x1_sp.ap()[g:g + 128, :]),
                reads=[f"x1_sp{u % 3}"], writes=[x1bn])
            for n in range(NNC):
                bank, bankn = pb[n % 3], f"pb{n % 3}"
                def mo_fn(t, j=j, n=n, bank=bank, nsu=nsu):
                    last = None
                    for ec in range(NCH):
                        last = t.matmul(
                            out=bank[:],
                            lhsT=nsu[:, ec * TSUP + j * 128: ec * TSUP + (j + 1) * 128],
                            rhs=woT[:, ec * D + n * 512: ec * D + (n + 1) * 512],
                            start=(ec == 0), stop=(ec == NCH - 1))
                    return last
                p.comp("pe", mo_fn, reads=[nsun, "qT"], writes=[bankn])
                p.comp("dve", lambda v, n=n, bank=bank, x1b=x1b: v.scalar_tensor_tensor(
                    out=x1b[:, n * 512:(n + 1) * 512], in0=bank[:], scalar=1.0,
                    in1=x1b[:, n * 512:(n + 1) * 512], op0=OP.bypass, op1=OP.add),
                    reads=[bankn, x1bn], writes=[x1bn])
            if "x2_tap" in debug_taps:
                p.dma("act", lambda s, g=g, x1b=x1b: s.dma_start(
                    out=x2_tap.ap()[g:g + 128, :], in_=x1b[:]), reads=[x1bn])

        def p3_B1(u):
            par = u % 2
            x1b, x1bn = x1p[par], f"x1p{par}"
            layer_norm(x1b, x1bn, 0)

        def p3_B2(u):
            s_i, j = divmod(u, NSUB)
            g = u * 128
            par = u % 2
            x1b, x1bn = x1p[par], f"x1p{par}"
            x3Tb, x3Tn = xn1T[par], f"xn1T{par}"
            transpose16(0, x3Tn,
                        lambda half, x3Tb=x3Tb: x3Tb[:, half * 1024:(half + 1) * 1024]
                        .rearrange("p (k w) -> p k w", w=128))
            for n in range(NNC):
                bank, bankn = pb[3 + n % 3], f"pb{3 + n % 3}"
                def mf_fn(t, n=n, bank=bank, x3Tb=x3Tb):
                    for kc in range(NCH):
                        t.matmul(out=bank[:], lhsT=x3Tb[:, kc * 128:(kc + 1) * 128],
                                 rhs=qT3[:, kc * D + n * 512: kc * D + (n + 1) * 512],
                                 start=(kc == 0), stop=False)
                    return t.matmul(out=bank[:], lhsT=ones_row[:],
                                    rhs=bias_bf[0:1, n * 512:(n + 1) * 512],
                                    start=False, stop=True)
                p.comp("pe", mf_fn, reads=[x3Tn, "wrT", "ones_row", "bias_bf"],
                       writes=[bankn])
                p.comp("dve", lambda v, n=n, bank=bank, x1b=x1b: v.scalar_tensor_tensor(
                    out=x1b[:, n * 512:(n + 1) * 512], in0=bank[:], scalar=0.5,
                    in1=x1b[:, n * 512:(n + 1) * 512], op0=OP.mult, op1=OP.add),
                    reads=[bankn, x1bn], writes=[x1bn])
            p.dma("act", lambda s, g=g, x1b=x1b: s.dma_start(
                out=out_ext[g:g + 128, :], in_=x1b[:]), reads=[x1bn], key="out")
            if "nrm_tap" in debug_taps and j == NSUB - 1:
                spar = s_i % 2
                p.dma("act", lambda s, s_i=s_i, spar=spar: s.dma_start(
                    out=nrm_tap.ap()[s_i], in_=nrm_su[spar][:]), reads=[f"nrm_su{spar}"])

        p3_nrm(0, list(range(NCH // 2)))
        p3_A(0)
        for u in range(NT):
            s_i, j = divmod(u, NSUB)
            if s_i + 1 < NSUP:
                p3_nrm(s_i + 1, list(range(j * 2, j * 2 + 2)))
            p3_B1(u)
            if u + 1 < NT:
                p3_A(u + 1)
            p3_B2(u)

        es3.close()
        block = E(nc.Block())
        p.finalize(block)

    return nc


def make_in_maps(inputs):
    x = np.asarray(inputs["L0_attn_x"], dtype=np.float32)
    wr = np.asarray(inputs["L0_attn_Wr"], dtype=np.float32)
    wk = np.asarray(inputs["L0_attn_Wk"], dtype=np.float32)
    wv = np.asarray(inputs["L0_attn_Wv"], dtype=np.float32)
    wo = np.asarray(inputs["L0_attn_Wo"], dtype=np.float32)
    decay = np.asarray(inputs["L0_attn_decay"], dtype=np.float32)
    gnw = np.asarray(inputs["L0_attn_gn_weight"], dtype=np.float32)
    gnb = np.asarray(inputs["L0_attn_gn_bias"], dtype=np.float32)
    wffn = np.asarray(inputs["L0_ffn_weight"], dtype=np.float32)
    bias = np.asarray(inputs["L0_ffn_bias"], dtype=np.float32)

    xr = np.ascontiguousarray(x.reshape(B, CPB, R, D))
    wrT = np.ascontiguousarray(wr.T)
    wkT = np.ascontiguousarray(wk.T)
    wvT = np.ascontiguousarray(wv.T)
    woT = np.ascontiguousarray(wo.T)
    gnw_c = np.ascontiguousarray(gnw.reshape(NCH, 128).T)
    gnb_c = np.ascontiguousarray(gnb.reshape(NCH, 128).T)
    decay_c = np.ascontiguousarray(decay.reshape(NCH, 128).T)
    bias_r = np.ascontiguousarray(bias.reshape(1, D))

    in_maps = []
    for c in range(NCORES):
        b, q = divmod(c, CPB)
        t0 = q * R
        pos = np.broadcast_to(
            (t0 + np.arange(R, dtype=np.float32)).reshape(1, R), (128, R))
        pm = np.zeros((128, NCORES), dtype=np.float32)
        for rk in range(b * CPB, b * CPB + q):
            pm[:, rk] = 1.0
        bm = np.zeros((2, NCORES), dtype=np.float32)
        bm[:, b * CPB:(b + 1) * CPB] = 1.0
        in_maps.append({
            "x": np.ascontiguousarray(xr[b, q]),
            "wrT": wrT, "wkT": wkT, "wvT": wvT, "woT": woT,
            "wffn": wffn, "bias": bias_r,
            "gnw": gnw_c, "gnb": gnb_c, "decay": decay_c,
            "pos": np.ascontiguousarray(pos), "pm": pm, "bm": bm,
        })
    return in_maps


def kernel(**inputs):
    nc = build_program()
    in_maps = make_in_maps(inputs)
    res = run_bass_kernel_spmd(nc, in_maps, core_ids=list(range(NCORES)))
    shards = [res.results[c]["out"] for c in range(NCORES)]
    out = np.stack(shards, axis=0).reshape(B, CPB, R, D).reshape(B, T, D)
    return np.ascontiguousarray(out.astype(np.float32))
